# version 17
# baseline (speedup 1.0000x reference)
import math
import os
import numpy as np
import ml_dtypes
from contextlib import ExitStack

import concourse.bass as bass
import concourse.mybir as mybir
from concourse.bass_utils import run_bass_kernel_spmd

F32 = mybir.dt.float32
BF16 = mybir.dt.bfloat16
AF = mybir.ActivationFunctionType
ALU = mybir.AluOpType
BF = ml_dtypes.bfloat16

B, T, D = 4, 4096, 1024
OWN = 2048
NH, HD = 8, 128
FF = 4096
E = 8
NEG = -1e9
W_ATT = 16
HALO = 32
SC = 512
NCH = OWN // SC
NT = 4
FCH = 512
NFC = OWN // FCH
HS = HALO + SC

SEM_NAMES = [
    "d_ld", "d_ldg", "d_x0", "d_x1", "d_xw0", "d_xw1", "d_xt", "d_xtf",
    "d_tw", "d_twp", "d_mwo", "d_mwop", "d_m10", "d_m11", "d_m20", "d_m21",
    "d_xr0", "d_xr1", "d_xa0", "d_xa1", "d_xf0", "d_xf1", "d_out0", "d_out1",
    "d_mb", "d_ts0", "d_ts1", "d_ts2", "d_ts3",
    "t", "v", "a", "p",
]


class Ctx:
    def __init__(self, sems):
        self.sems = sems
        self.eng_name = None
        self.eng = None
        self.cnt = {k: 0 for k in sems}

    def reset(self, name, eng):
        self.eng_name = name
        self.eng = eng
        for k in self.cnt:
            self.cnt[k] = 0

    def on(self, name):
        return self.eng_name == name

    def inc(self, sname, instr, k=1):
        if instr is not None:
            instr.then_inc(self.sems[sname], k)
        self.cnt[sname] += k

    def wait(self, who, *ss):
        if self.on(who):
            for sname in ss:
                if self.cnt[sname] > 0:
                    self.eng.wait_ge(self.sems[sname], self.cnt[sname])

    def wait_at(self, who, sname, val):
        if self.on(who) and val > 0:
            self.eng.wait_ge(self.sems[sname], val)

    def mm(self, out, lhsT, rhs, start, stop, skip=False):
        if self.on("tensor"):
            i = self.eng.matmul(out, lhsT, rhs, start=start, stop=stop,
                                skip_group_check=skip)
        else:
            i = None
        self.inc("t", i)

    def trans(self, out, in_, ident):
        i = self.eng.transpose(out, in_, ident) if self.on("tensor") else None
        self.inc("t", i)

    def act(self, out, in_, func, **kw):
        i = self.eng.activation(out, in_, func, **kw) if self.on("scalar") else None
        self.inc("a", i)

    def vop(self, name, *args, who="vector", **kw):
        sem = "v" if who == "vector" else "p"
        i = getattr(self.eng, name)(*args, **kw) if self.on(who) else None
        self.inc(sem, i)

    def dma(self, q, out, in_, sem, transpose=False):
        if self.on(q):
            i = self.eng.dma_start(out=out, in_=in_, transpose=transpose)
        else:
            i = None
        self.inc(sem, i, 16)


def build_program():
    nc = bass.Bass("TRN2", target_bir_lowering=False, dynamic_dma_scratch_size=8192)

    ap = {}
    def din(name, shape, dtype=F32):
        ap[name] = nc.dram_tensor(name, list(shape), dtype, kind="ExternalInput").ap()
    din("x", (T, D))
    din("xhalo", (HALO, D))
    din("wq", (D, D)); din("wk", (D, D)); din("wv", (D, D)); din("wo", (D, D))
    din("qwc", (128, 8)); din("kwc", (128, 8))
    din("gw1tc", (128, 8 * D)); din("gw2tc", (128, 8 * E))
    din("gb1", (1, D)); din("gb2r", (1, E))
    din("W1", (FF, D)); din("W2", (D, FF))
    din("TW1", (E, FF, D)); din("TW2", (E, D, FF))
    din("b1c", (128, 32)); din("b2c", (128, 8))
    din("Tb1c", (128, 8 * 32)); din("Tb2c", (128, 8 * 8))
    din("identbf", (128, 128), BF16)
    din("onesbf", (128, 128), BF16); din("onesf", (128, 128), F32)
    din("e0bf", (128, 128), BF16); din("e0f", (128, 128), F32)
    din("idhal", (128, HALO), BF16); din("ohalbf", (128, 1), BF16)
    din("selbf", (128, HALO), BF16)
    din("biasm", (128, NH * 128), BF16)
    din("epsc", (128, 2))
    din("epsc3", (128, 1))
    din("bhal", (128, NH * 128), BF16)
    din("bhal0", (128, NH * 128), BF16)
    ap["xnd"] = nc.dram_tensor("xnd", [HALO + T, D], BF16, kind="Internal").ap()
    ap["m1d"] = nc.dram_tensor("m1d", [FF, D], BF16, kind="Internal").ap()
    ap["m2d"] = nc.dram_tensor("m2d", [D, FF], BF16, kind="Internal").ap()
    ap["xad"] = nc.dram_tensor("xad", [OWN, D], F32, kind="Internal").ap()
    ap["mbd"] = nc.dram_tensor("mbd", [1, D], F32, kind="Internal").ap()
    ap["out"] = nc.dram_tensor("out", [OWN, D], F32, kind="ExternalOutput").ap()

    stack = ExitStack()
    sbuf = lambda name, shape, dtype: stack.enter_context(
        nc.sbuf_tensor(name + "_s", list(shape), dtype))[:]
    psum = lambda name, shape: stack.enter_context(
        nc.psum_tensor(name, list(shape), F32))[:]

    s = {}
    for n, sh, dt in [
        ("identbf", (128, 128), BF16),
        ("onesbf", (128, 128), BF16), ("onesf", (128, 128), F32),
        ("e0bf", (128, 128), BF16), ("e0f", (128, 128), F32),
        ("idhal", (128, HALO), BF16), ("ohalbf", (128, 1), BF16),
        ("selbf", (128, HALO), BF16), ("biasm", (128, NH * 128), BF16),
        ("bhal", (128, NH * 128), BF16), ("bhal0", (128, NH * 128), BF16),
        ("gw2t", (128, 8 * E), BF16), ("qwc", (128, 8), F32), ("kwc", (128, 8), F32),
        ("gb1r", (128, D), BF16), ("gb2r", (1, E), F32),
        ("b1c", (128, 32), F32), ("b2c", (128, 8), F32),
        ("Tb1c", (128, 8 * 32), F32), ("Tb2c", (128, 8 * 8), F32),
        ("grow", (128, E), F32), ("m2row", (1, D), F32), ("m2rowb", (128, D), BF16),
        ("rrows", (128, 1024), F32),
        ("gbc", (128, E), F32), ("mb1", (128, 32), F32), ("mb2", (128, 8), F32),
        ("hbarT", (128, 8), BF16), ("hbacc", (128, 8), F32),
        ("stats", (128, 12), F32), ("xnb", (128, 2 * D), BF16),
        ("junk1", (128, D), F32),
        ("hrelu", (128, D), BF16),
        ("xt", (128, 8 * HS), BF16),
        ("raw", (128, 8 * SC), F32), ("hraw", (128, 8 * HALO), F32),
        ("sq", (128, 2 * SC), BF16), ("sqh", (128, HALO), BF16),
        ("rbq", (128, SC), F32), ("rbk", (128, SC), F32), ("rbh", (128, HALO), F32),
        ("rz", (128, NH * 128), F32),
        ("vhal", (128, D), BF16),
        ("exps", (128, NH * 128), BF16), ("expsh", (128, NH * 128), BF16),
        ("yt", (128, NH * 128), BF16),
        ("accv", (128, D), F32), ("mwov", (128, D), BF16),
        ("accv2", (128, D), F32), ("mwov2", (128, D), BF16),
    ]:
        s[n] = sbuf(n, sh, dt)
    s["tmp1"] = s["junk1"][0:1, 0:SC]
    s["rqrow"] = s["rrows"][0:1, 0:SC]
    s["rkrow"] = s["rrows"][0:1, SC:2 * SC]
    s["epsc"] = sbuf("epsc", (128, 2), F32)
    nc.const_aps.register(F32, 1e-5, s["epsc"][:, 0:1])
    nc.const_aps.register(F32, 1e-6, s["epsc"][:, 1:2])
    LNB = -0.5 * math.log(HD)
    s["epsc3"] = sbuf("epsc3", (128, 1), F32)
    nc.const_aps.register(F32, LNB, s["epsc3"][:, 0:1])

    O1 = sbuf("O1", (128, 32768), BF16)
    s["wq"] = O1[:, 0:8192]; s["wk"] = O1[:, 8192:16384]
    s["wv"] = O1[:, 16384:24576]; s["wo"] = O1[:, 24576:32768]
    s["u"] = O1[:, 0:32 * FCH]
    s["m1t"] = O1[:, 16384:24576]
    s["xtf"] = O1[:, 24576:24576 + 8 * FCH]
    s["m2t"] = O1[:, 28672:28672 + 2 * D]
    O2 = sbuf("O2", (128, 8192), BF16)
    s["xband"] = O2.bitcast(F32)
    s["twv"] = O2.bitcast(F32)

    O5 = sbuf("O5", (128, 8704), BF16)
    s["gw1t"] = O5[:, 0:8192]
    s["qt"] = O5[:, 0:8 * SC]
    s["kt"] = O5[:, 4096:4096 + 8 * HS]
    O6 = sbuf("O6", (128, 2 * D), F32)
    s["xr"] = O6
    s["xf"] = O6
    O7 = sbuf("O7", (128, 2 * D), F32)
    s["xattn"] = O7
    O8 = sbuf("O8", (128, 2 * D), F32)
    s["v"] = O8.bitcast(BF16)
    s["outb"] = O8

    pA = psum("pA", (128, 512))
    pB = psum("pB", (128, 512))
    pC = psum("pC", (128, 512))
    pD = psum("pD", (128, 512))
    pE0 = psum("pE0", (128, 512))
    pE1 = psum("pE1", (128, 512))
    pU = psum("pU", (128, 1024))

    sems = {n: stack.enter_context(nc.semaphore(n)) for n in SEM_NAMES}
    c = Ctx(sems)
    bk = {}

    def W(name, kt, hd):
        return s[name][:, kt * D + hd * 128: kt * D + (hd + 1) * 128]

    P0SUB = int(os.environ.get("K_P0SUB", "99"))

    def emit_p0(cx):
        if P0SUB < 1:
            return
        for n in ("identbf", "onesbf", "onesf", "selbf", "e0bf", "e0f",
                  "idhal", "ohalbf",
                  "biasm", "bhal", "bhal0", "gb2r", "Tb1c", "Tb2c", "epsc", "epsc3",
                  "qwc", "kwc", "b1c", "b2c"):
            cx.dma("sync", s[n], ap[n], "d_ld")
        if P0SUB == 1:
            return
        for n in ("wq", "wk", "wv", "wo"):
            cx.dma("gpsimd", s[n].rearrange("p (t d) -> p t d", d=D),
                   ap[n].rearrange("(t p) d -> p t d", p=128), "d_ldg")
        cx.dma("gpsimd", s["gw1t"].rearrange("p (t d) -> p t d", d=D),
               ap["gw1tc"].rearrange("p (t d) -> p t d", d=D), "d_ldg")
        cx.dma("gpsimd", s["gw2t"], ap["gw2tc"], "d_ldg")
        for nm in ("vhal", "gb1r", "rrows", "grow", "rz", "m2rowb", "expsh"):
            cx.vop("memset", s[nm], 0.0)
        bk["p0ms_v"] = cx.cnt["v"]
        cx.wait_at("gpsimd", "v", bk["p0ms_v"])
        cx.dma("gpsimd", s["gb1r"][0:1, :], ap["gb1"], "d_ldg")
        if P0SUB < 40:
            return
        cx.dma("sync", s["raw"][0:HALO, 1024:2048], ap["xhalo"], "d_mb")
        src = s["raw"][0:HALO, 1024:2048]
        jout = s["raw"][0:HALO, 3072:4096]
        scol = 6
        cx.wait("vector", "d_ld", "d_mb")
        cx.vop("tensor_scalar", jout, src, 1.0 / D, 0.0,
               op0=ALU.mult, op1=ALU.add,
               accum_out=s["stats"][0:HALO, scol:scol + 1])
        cx.wait("scalar", "v", "d_ld", "d_mb")
        cx.act(jout, src, AF.Square, scale=1.0 / 32.0,
               accum_out=s["stats"][0:HALO, scol + 1:scol + 2])
        cx.wait("vector", "a")
        cx.vop("scalar_tensor_tensor", s["stats"][0:HALO, scol + 2:scol + 3],
               s["stats"][0:HALO, scol:scol + 1], s["stats"][0:HALO, scol:scol + 1],
               s["stats"][0:HALO, scol + 1:scol + 2], op0=ALU.mult, op1=ALU.subtract)
        cx.wait("scalar", "v")
        cx.act(s["stats"][0:HALO, scol + 3:scol + 4], s["stats"][0:HALO, scol + 2:scol + 3],
               AF.Ln, scale=-1.0, bias=1e-5)
        cx.wait("scalar", "a")
        cx.act(s["stats"][0:HALO, scol + 4:scol + 5], s["stats"][0:HALO, scol + 3:scol + 4],
               AF.Exp, scale=-0.5)
        cx.wait("vector", "a")
        cx.vop("tensor_scalar", s["xnb"][0:HALO, D:2 * D], src,
               s["stats"][0:HALO, scol:scol + 1], s["stats"][0:HALO, scol + 4:scol + 5],
               op0=ALU.subtract, op1=ALU.mult)
        cx.wait("scalar", "v")
        cx.dma("scalar", ap["xnd"][0:HALO, :], s["xnb"][0:HALO, D:2 * D], "d_xw1")
        bk["xw_d"].append(("d_xw1", cx.cnt["d_xw1"]))

    def emit_ln_tile(cx, tile, gt, bpar):
        scol = 6 * (gt % 2)
        dx = "d_x%d" % bpar
        cx.wait("vector", dx, "a")
        cx.vop("tensor_scalar", s["junk1"], tile, 1.0 / D, 0.0,
               op0=ALU.mult, op1=ALU.add, accum_out=s["stats"][:, scol:scol + 1])
        cx.wait("scalar", "v", dx, "d_ld")
        cx.act(s["junk1"], tile, AF.Square, scale=1.0 / 32.0,
               accum_out=s["stats"][:, scol + 1:scol + 2])
        cx.wait("vector", "a")
        cx.vop("scalar_tensor_tensor", s["stats"][:, scol + 2:scol + 3],
               s["stats"][:, scol:scol + 1], s["stats"][:, scol:scol + 1],
               s["stats"][:, scol + 1:scol + 2], op0=ALU.mult, op1=ALU.subtract)
        cx.wait("scalar", "v")
        cx.act(s["stats"][:, scol + 3:scol + 4], s["stats"][:, scol + 2:scol + 3],
               AF.Ln, scale=-1.0, bias=1e-5)
        cx.wait("scalar", "a")
        cx.act(s["stats"][:, scol + 4:scol + 5], s["stats"][:, scol + 3:scol + 4],
               AF.Exp, scale=-0.5)
        cx.wait("vector", "a")
        if len(bk["xw_d"]) >= 2:
            psem, pval = bk["xw_d"][-2]
            cx.wait_at("vector", psem, pval)
        cx.vop("tensor_scalar", s["xnb"][:, (gt % 2) * D:((gt % 2) + 1) * D], tile,
               s["stats"][:, scol:scol + 1], s["stats"][:, scol + 4:scol + 5],
               op0=ALU.subtract, op1=ALU.mult)
        cx.wait("scalar", "v")
        xwsem = "d_xw%d" % (gt % 2)
        cx.dma("scalar", ap["xnd"][HALO + gt * 128: HALO + (gt + 1) * 128, :],
               s["xnb"][:, (gt % 2) * D:((gt % 2) + 1) * D], xwsem)
        bk["xw_d"].append((xwsem, cx.cnt[xwsem]))

    def emit_ln_gate(cx):
        for gb in range(int(os.environ.get("K_STOPGB", "8"))):
            for half in range(2):
                band = gb * 2 + half
                bpar = band % 2
                if band >= 2:
                    cx.wait_at("sync", "v", bk["xband_v"][band - 2])
                    cx.wait_at("sync", "a", bk["xband_a"][band - 2])
                cx.dma("sync",
                       s["xband"][:, bpar * 2048:(bpar + 1) * 2048].rearrange(
                           "p (t d) -> p t d", d=D),
                       ap["x"][band * 256:(band + 1) * 256, :].rearrange(
                           "(t p) d -> p t d", p=128), "d_x%d" % bpar)
                for st in range(2):
                    tile = s["xband"][:, bpar * 2048 + st * D: bpar * 2048 + (st + 1) * D]
                    emit_ln_tile(cx, tile, band * 2 + st, bpar)
                bk["xband_v"].append(cx.cnt["v"])
                bk["xband_a"].append(cx.cnt["a"])
            if gb >= 1:
                cx.wait_at("scalar", "t", bk["xt_gate_t"][-1])
            cx.wait("scalar", "d_xw0", "d_xw1")
            for hc in range(8):
                cx.dma("scalar", s["xt"][:, hc * HS:(hc + 1) * HS],
                       ap["xnd"][gb * SC: gb * SC + HS, hc * 128:(hc + 1) * 128],
                       "d_xt", transpose=True)
            for st in range(4):
                cx.wait("tensor", "d_xt", "d_ld", "d_ldg")
                cx.wait_at("tensor", "v", bk["p0ms_v"])
                if gb or st:
                    cx.wait_at("tensor", "a", bk["relu_a"][-1])
                KG = os.environ.get("K_GATE", "")
                for half, pX in ((0, pC), (1, pD)):
                    if KG != "nobias":
                        cx.mm(pX, s["e0bf"],
                              s["gb1r"][:, half * 512:(half + 1) * 512],
                              start=True, stop=False)
                    for kt in range(8):
                        rhs = (s["gw1t"][:, 0:512] if KG == "fixrhs" else
                               s["gw1t"][:, kt * D + half * 512: kt * D + (half + 1) * 512])
                        lhs = (s["xt"][:, HALO:HALO + 128] if KG == "fixlhs" else
                               s["xt"][:, kt * HS + HALO + st * 128:
                                       kt * HS + HALO + (st + 1) * 128])
                        cx.mm(pX, lhs, rhs,
                              start=(kt == 0 and KG == "nobias"), stop=(kt == 7))
                cx.wait("scalar", "t")
                if bk["hbar_t"]:
                    cx.wait_at("scalar", "t", bk["hbar_t"][-1])
                cx.act(s["hrelu"][:, 0:512], pC, AF.Relu)
                cx.act(s["hrelu"][:, 512:1024], pD, AF.Relu)
                bk["relu_a"].append(cx.cnt["a"])
                cx.wait("tensor", "a")
                tj = gb * 4 + st
                if os.environ.get("K_NOHBAR", "") != "1":
                    for cb in range(8):
                        cx.mm(pE0[:, cb * 32 + tj: cb * 32 + tj + 1],
                              s["hrelu"][:, cb * 128:(cb + 1) * 128],
                              s["onesbf"][:, 0:1], start=True, stop=True)
                bk["hbar_t"].append(cx.cnt["t"])
            bk["xt_gate_t"].append(cx.cnt["t"])
        for _ in range(2):
            cx.mm(pE0[:, 300:301], s["onesbf"], s["onesbf"][:, 0:1],
                  start=True, stop=True)
        cx.wait("vector", "t")
        for cb in range(8):
            cx.vop("tensor_scalar", s["junk1"][:, cb * 32:(cb + 1) * 32],
                   pE0[:, cb * 32:(cb + 1) * 32], 1.0, 0.0,
                   op0=ALU.mult, op1=ALU.add,
                   accum_out=s["hbacc"][:, cb:cb + 1])
        bk["hbred_v"] = cx.cnt["v"]

    def emit_g(cx):
        cx.wait("scalar", "t", "v")
        cx.act(s["hbarT"], s["hbacc"], AF.Identity, scale=1.0 / T)
        cx.wait("tensor", "a")
        for ot in range(8):
            cx.mm(pA[0:1, 0:E], s["hbarT"][:, ot:ot + 1],
                  s["gw2t"][:, ot * E:(ot + 1) * E], start=(ot == 0), stop=(ot == 7))
        cx.wait("vector", "t")
        cx.vop("tensor_tensor", s["grow"][0:1, :], pA[0:1, 0:E], s["gb2r"], op=ALU.add)
        cx.wait("tensor", "v")
        cx.mm(pU[:, 64:64 + E], s["e0f"], s["grow"], start=True, stop=True)
        cx.wait("scalar", "t")
        cx.act(s["gbc"], pU[:, 64:64 + E], AF.Copy)
        bk["gbc_a"] = cx.cnt["a"]
        cx.wait("vector", "a")
        cx.vop("tensor_copy", s["mb1"], s["b1c"])
        cx.vop("tensor_copy", s["mb2"], s["b2c"])
        for e in range(E):
            cx.wait("vector", "v")
            cx.vop("scalar_tensor_tensor", s["mb1"], s["Tb1c"][:, e * 32:(e + 1) * 32],
                   s["gbc"][:, e:e + 1], s["mb1"], op0=ALU.mult, op1=ALU.add)
            cx.wait("vector", "v")
            cx.vop("scalar_tensor_tensor", s["mb2"], s["Tb2c"][:, e * 8:(e + 1) * 8],
                   s["gbc"][:, e:e + 1], s["mb2"], op0=ALU.mult, op1=ALU.add)
        cx.wait("scalar", "v")
        for t in range(8):
            cx.dma("scalar", ap["mbd"][0:1, t * 128:(t + 1) * 128].rearrange("a p -> p a"),
                   s["mb2"][:, t:t + 1], "d_mb")
        cx.wait("sync", "d_mb")
        cx.dma("sync", s["m2row"], ap["mbd"], "d_mb")
        cx.wait("vector", "d_mb")
        cx.wait("vector", "v")
        cx.vop("tensor_copy", s["m2rowb"][0:1, :], s["m2row"])
        bk["m2row_v"] = cx.cnt["v"]
        bk["g_done_t"] = cx.cnt["t"]

    BLD_ORDER = list(range(32)) + [32 + ht * 4 + fq
                                   for fq in range(4) for ht in range(8)]

    def emit_build_group(cx, g):
        m1 = g < 32
        q = "sync" if m1 else "scalar"
        if m1:
            ft = g
            wsrc = ap["W1"][ft * 128:(ft + 1) * 128, :]
            twsrc = ap["TW1"][:, ft * 128:(ft + 1) * 128, :]
            dst = ap["m1d"][ft * 128:(ft + 1) * 128, :]
        else:
            ht, fq = (g - 32) // 4, (g - 32) % 4
            wsrc = ap["W2"][ht * 128:(ht + 1) * 128, fq * D:(fq + 1) * D]
            twsrc = ap["TW2"][:, ht * 128:(ht + 1) * 128, fq * D:(fq + 1) * D]
            dst = ap["m2d"][ht * 128:(ht + 1) * 128, fq * D:(fq + 1) * D]
        par = bk["bld_n"] % 2
        acc = s["accv"] if par == 0 else s["accv2"]
        outbuf = s["mwov"] if par == 0 else s["mwov2"]
        wdsem = "d_tw" if par == 0 else "d_twp"
        osem = "d_mwo" if par == 0 else "d_mwop"
        slots = [s["twv"][:, r * D:(r + 1) * D] for r in range(4)]
        cx.wait_at(q, "v", bk["acc_done"][par])
        cx.dma(q, acc, wsrc, wdsem)
        wsem, wcnt = wdsem, cx.cnt[wdsem]
        i0 = bk["bld_idx"]
        v0 = cx.cnt["v"]
        for e in range(8):
            i = i0 + e
            r = i % 4
            if i < 4:
                cx.wait_at(q, "v", bk["pa_end_v"])
                cx.wait_at(q, "a", bk["pa_end_a"])
            elif e < 4:
                cx.wait_at(q, "v", bk["mac_cnt"][i - 4])
            else:
                cx.wait_at(q, "v", v0 + (e - 4) + 1)
            ssem = "d_ts%d" % r
            cx.dma(q, slots[r], twsrc[e], ssem)
            bk["slot_dma"][i] = (ssem, cx.cnt[ssem])
        for e in range(8):
            i = i0 + e
            ds, dc = bk["slot_dma"][i]
            cx.wait_at("vector", ds, dc)
            if e == 0:
                cx.wait_at("vector", wsem, wcnt)
                cx.wait_at("vector", "a", bk["gbc_a"])
            cx.wait("vector", "v")
            out = acc if e < 7 else outbuf
            if e == 7 and bk["out_sem"][par] is not None:
                osm, ocn = bk["out_sem"][par]
                cx.wait_at("vector", osm, ocn)
            cx.vop("scalar_tensor_tensor", out, slots[i % 4],
                   s["gbc"][:, e:e + 1], acc, op0=ALU.mult, op1=ALU.add)
            bk["mac_cnt"][i] = cx.cnt["v"]
        bk["bld_idx"] = i0 + 8
        done = cx.cnt["v"]
        bk["acc_done"][par] = done
        cx.wait_at(q, "v", done)
        cx.dma(q, dst, outbuf, osem)
        bk["out_sem"][par] = (osem, cx.cnt[osem])
        bk["bld_n"] += 1
        bk["group_out"][g] = (cx.cnt["d_mwo"], cx.cnt["d_mwop"])

    def emit_vector_builds(cx, n, only_m1=False):
        done = 0
        while done < n and bk["vq"] < 64:
            if only_m1 and BLD_ORDER[bk["vq"]] >= 32:
                return
            emit_build_group(cx, BLD_ORDER[bk["vq"]])
            bk["vq"] += 1
            done += 1

    def emit_attn(cx):
        cx.wait_at("tensor", "v", bk.get("hbred_v", 0))
        bk["xt_war"] = cx.cnt["t"]
        bk["qt_war"] = bk["g_done_t"]
        bk["kt_war"] = bk["g_done_t"]
        for ci in range(NCH):
            sc0 = ci * SC
            cx.wait_at("scalar", "t", bk["xt_war"])
            cx.wait("scalar", "d_xw0", "d_xw1")
            for hc in range(8):
                cx.dma("scalar", s["xt"][:, hc * HS:(hc + 1) * HS],
                       ap["xnd"][sc0: sc0 + HS, hc * 128:(hc + 1) * 128],
                       "d_xt", transpose=True)
            if ci == 0:
                emit_halo_prologue(cx)
            else:
                cx.wait_at("vector", "t", bk["kt_war"])
                cx.wait("vector", "v")
                for hd in range(NH):
                    base = hd * HS
                    cx.vop("tensor_copy", s["kt"][:, base: base + HALO],
                           s["kt"][:, base + SC: base + SC + HALO])
                cx.wait("tensor", "v", "a")
                cx.mm(pA[0:HALO, :], s["selbf"], s["v"][:, 3 * D: 3 * D + 512],
                      start=True, stop=True)
                cx.mm(pB[0:HALO, :], s["selbf"], s["v"][:, 3 * D + 512: 4 * D],
                      start=True, stop=True)
                cx.wait("scalar", "t")
                cx.act(s["vhal"][0:HALO, 0:512], pA[0:HALO, :], AF.Copy)
                bk["ppa"].append(cx.cnt["a"])
                cx.act(s["vhal"][0:HALO, 512:1024], pB[0:HALO, :], AF.Copy)
                bk["ppa"].append(cx.cnt["a"])
            emit_qkv(cx, ci)
            bk["xt_war"] = cx.cnt["t"]
            for j in range(NT):
                gt = ci * NT + j
                if j > 0:
                    cx.wait("tensor", "a")
                    cx.mm(pA[0:HALO, :], s["selbf"], s["v"][:, (j - 1) * D:(j - 1) * D + 512],
                          start=True, stop=True)
                    cx.mm(pB[0:HALO, :], s["selbf"], s["v"][:, (j - 1) * D + 512: j * D],
                          start=True, stop=True)
                    cx.wait("scalar", "t")
                    cx.act(s["vhal"][0:HALO, 0:512], pA[0:HALO, :], AF.Copy)
                    cx.act(s["vhal"][0:HALO, 512:1024], pB[0:HALO, :], AF.Copy)
                emit_attn_tile(cx, j, gt)
                if gt >= 1:
                    emit_vector_builds(cx, 2, only_m1=True)
            bk["qt_war"] = cx.cnt["t"]
            bk["kt_war"] = cx.cnt["t"]

    def emit_halo_prologue(cx):
        cx.wait("tensor", "d_xt", "d_ld")
        cx.wait_at("tensor", "v", bk["p0ms_v"])
        cx.wait_at("scalar", "v", bk["p0ms_v"])
        cx.wait_at("tensor", "t", bk["g_done_t"])
        for hd in range(NH):
            pX = pA if hd % 2 == 0 else pB
            if hd >= 2:
                cx.wait_at("tensor", "a", bk["hraw_a"][hd - 2])
            for kt in range(8):
                cx.mm(pX[:, 0:HALO], W("wk", kt, hd),
                      s["xt"][:, kt * HS: kt * HS + HALO],
                      start=(kt == 0), stop=(kt == 7))
            cx.wait("scalar", "t")
            cx.act(s["hraw"][:, hd * HALO:(hd + 1) * HALO], pX[:, 0:HALO], AF.Copy)
            bk["ppa"].append(cx.cnt["a"])
            cx.wait("scalar", "a")
            cx.act(s["sqh"], s["hraw"][:, hd * HALO:(hd + 1) * HALO], AF.Square,
                   scale=1.0 / 32.0)
            bk["hraw_a"].append(cx.cnt["a"])
            cx.wait("tensor", "a")
            cx.mm(pE1[0:1, 0:HALO], s["onesbf"][:, 0:1], s["sqh"][:, 0:HALO],
                  start=(hd == 0), stop=(hd == 7))
        cx.wait("scalar", "t")
        cx.act(s["tmp1"][0:1, 0:HALO], pE1[0:1, 0:HALO], AF.Ln, bias=1e-6)
        cx.wait("scalar", "a")
        cx.act(s["rqrow"][0:1, 0:HALO], s["tmp1"][0:1, 0:HALO], AF.Exp, scale=-0.5)
        cx.wait("tensor", "a")
        cx.mm(pU[:, 0:HALO], s["e0f"], s["rrows"][:, 0:HALO],
              start=True, stop=True)
        cx.wait("scalar", "t")
        cx.act(s["rbh"], pU[:, 0:HALO], AF.Copy)
        cx.wait("vector", "a")
        for hd in range(NH):
            cx.vop("scalar_tensor_tensor", s["kt"][:, hd * HS: hd * HS + HALO],
                   s["hraw"][:, hd * HALO:(hd + 1) * HALO], s["kwc"][:, hd:hd + 1],
                   s["rbh"], op0=ALU.mult, op1=ALU.mult)
        cx.wait("tensor", "v", "a")
        for half, pX in ((0, pA), (1, pB)):
            for kt in range(8):
                cx.mm(pX[0:HALO, :], s["xt"][:, kt * HS: kt * HS + HALO],
                      s["wv"][:, kt * D + half * 512: kt * D + (half + 1) * 512],
                      start=(kt == 0), stop=(kt == 7))
        cx.wait("scalar", "t")
        cx.act(s["vhal"][0:HALO, 0:512], pA[0:HALO, :], AF.Copy)
        bk["ppa"].append(cx.cnt["a"])
        cx.act(s["vhal"][0:HALO, 512:1024], pB[0:HALO, :], AF.Copy)
        bk["ppa"].append(cx.cnt["a"])

    def emit_qkv(cx, ci):
        cx.wait("tensor", "d_xt", "a")
        cx.wait_at("tensor", "v", bk["p0ms_v"])
        cx.wait_at("scalar", "v", bk["p0ms_v"])
        if bk["xattn_v"]:
            cx.wait_at("tensor", "v", bk["xattn_v"][-1])
        qstt = []
        for hd in range(NH):
            pX = pA if hd % 2 == 0 else pB
            if len(bk["ppa"]) >= 2:
                cx.wait_at("tensor", "a", bk["ppa"][-2])
            for kt in range(8):
                cx.mm(pX, W("wq", kt, hd), s["xt"][:, kt * HS + HALO: kt * HS + HS],
                      start=(kt == 0), stop=(kt == 7))
            cx.wait("scalar", "t")
            if bk["kstt_v"]:
                cx.wait_at("scalar", "v", bk["kstt_v"][-1])
            cx.act(s["raw"][:, hd * SC:(hd + 1) * SC], pX, AF.Copy)
            bk["ppa"].append(cx.cnt["a"])
            cx.wait("scalar", "a")
            cx.act(s["sq"][:, (hd % 2) * SC:((hd % 2) + 1) * SC],
                   s["raw"][:, hd * SC:(hd + 1) * SC], AF.Square, scale=1.0 / 32.0)
            bk["qraw_a"].append(cx.cnt["a"])
            cx.wait("tensor", "a")
            cx.mm(pE0[0:1, :], s["onesbf"][:, 0:1],
                  s["sq"][:, (hd % 2) * SC:((hd % 2) + 1) * SC],
                  start=(hd == 0), stop=(hd == 7))
        cx.wait("scalar", "t")
        cx.act(s["tmp1"], pE0[0:1, :], AF.Ln, bias=1e-6)
        cx.wait("scalar", "a")
        cx.act(s["rqrow"], s["tmp1"], AF.Exp, scale=-0.5, bias=LNB)
        cx.wait("tensor", "a")
        cx.mm(pU[:, 0:512], s["e0f"], s["rrows"][:, 0:SC], start=True, stop=True)
        cx.wait("scalar", "t")
        cx.act(s["rbq"], pU[:, 0:512], AF.Copy)
        cx.wait_at("vector", "t", bk["qt_war"])
        cx.wait("vector", "a")
        for hd in range(NH):
            cx.vop("scalar_tensor_tensor", s["qt"][:, hd * SC:(hd + 1) * SC],
                   s["raw"][:, hd * SC:(hd + 1) * SC], s["qwc"][:, hd:hd + 1], s["rbq"],
                   op0=ALU.mult, op1=ALU.mult)
            qstt.append(cx.cnt["v"])
        for hd in range(NH):
            pX = pA if hd % 2 == 0 else pB
            if len(bk["ppa"]) >= 2:
                cx.wait_at("tensor", "a", bk["ppa"][-2])
            for kt in range(8):
                cx.mm(pX, W("wk", kt, hd), s["xt"][:, kt * HS + HALO: kt * HS + HS],
                      start=(kt == 0), stop=(kt == 7))
            cx.wait("scalar", "t")
            cx.wait_at("scalar", "v", qstt[hd])
            cx.act(s["raw"][:, hd * SC:(hd + 1) * SC], pX, AF.Copy)
            bk["ppa"].append(cx.cnt["a"])
            cx.wait("scalar", "a")
            cx.act(s["sq"][:, (hd % 2) * SC:((hd % 2) + 1) * SC],
                   s["raw"][:, hd * SC:(hd + 1) * SC], AF.Square, scale=1.0 / 32.0)
            cx.wait("tensor", "a")
            cx.mm(pE0[0:1, :], s["onesbf"][:, 0:1],
                  s["sq"][:, (hd % 2) * SC:((hd % 2) + 1) * SC],
                  start=(hd == 0), stop=(hd == 7))
        cx.wait("scalar", "t")
        cx.act(s["tmp1"], pE0[0:1, :], AF.Ln, bias=1e-6)
        cx.wait("scalar", "a")
        cx.act(s["rkrow"], s["tmp1"], AF.Exp, scale=-0.5)
        cx.wait("tensor", "a")
        cx.mm(pU[:, 512:1024], s["e0f"], s["rrows"][:, SC:2 * SC], start=True, stop=True)
        cx.wait("scalar", "t")
        cx.act(s["rbk"], pU[:, 512:1024], AF.Copy)
        cx.wait_at("vector", "t", bk["kt_war"])
        cx.wait("vector", "a")
        for hd in range(NH):
            cx.vop("scalar_tensor_tensor", s["kt"][:, hd * HS + HALO: hd * HS + HS],
                   s["raw"][:, hd * SC:(hd + 1) * SC], s["kwc"][:, hd:hd + 1], s["rbk"],
                   op0=ALU.mult, op1=ALU.mult)
            bk["kstt_v"].append(cx.cnt["v"])
        for j in range(NT):
            for half, pX in ((0, pA), (1, pB)):
                if len(bk["ppa"]) >= 2:
                    cx.wait_at("tensor", "a", bk["ppa"][-2])
                for kt in range(8):
                    cx.mm(pX, s["xt"][:, kt * HS + HALO + j * 128:
                                      kt * HS + HALO + (j + 1) * 128],
                          s["wv"][:, kt * D + half * 512: kt * D + (half + 1) * 512],
                          start=(kt == 0), stop=(kt == 7))
                cx.wait("scalar", "t")
                cx.act(s["v"][:, j * D + half * 512: j * D + (half + 1) * 512], pX, AF.Copy)
                bk["ppa"].append(cx.cnt["a"])

    def emit_attn_tile(cx, j, gt):
        bh = s["bhal0"] if gt == 0 else s["bhal"]
        if j == 0 and bk["kstt_v"]:
            cx.wait_at("tensor", "v", bk["kstt_v"][-1])
        if bk["xattn_v"]:
            cx.wait_at("tensor", "v", bk["xattn_v"][-1])
        if bk["tile_a"]:
            cx.wait_at("tensor", "a", bk["tile_a"][-1])
        cx.mm(pC, s["identbf"], s["biasm"][:, 0:512], start=True, stop=False)
        cx.mm(pD, s["identbf"], s["biasm"][:, 512:1024], start=True, stop=False)
        cx.mm(pE0[0:HALO, :], s["idhal"], bh[:, 0:512],
              start=True, stop=False)
        cx.mm(pE1[0:HALO, :], s["idhal"], bh[:, 512:1024],
              start=True, stop=False)
        for hd in range(NH):
            pX = pC if hd < 4 else pD
            pH = pE0 if hd < 4 else pE1
            col = (hd % 4) * 128
            cx.mm(pX[:, col:col + 128],
                  s["kt"][:, hd * HS + HALO + j * 128: hd * HS + HALO + (j + 1) * 128],
                  s["qt"][:, hd * SC + j * 128: hd * SC + (j + 1) * 128],
                  start=False, stop=(hd % 4 == 3))
            cx.mm(pH[0:HALO, col:col + 128],
                  s["kt"][:, hd * HS + j * 128: hd * HS + j * 128 + HALO],
                  s["qt"][:, hd * SC + j * 128: hd * SC + (j + 1) * 128],
                  start=False, stop=(hd % 4 == 3))
        cx.wait("scalar", "t")
        cx.act(s["exps"][:, 0:512], pC, AF.Exp)
        cx.act(s["exps"][:, 512:1024], pD, AF.Exp)
        cx.act(s["expsh"][0:HALO, 0:512], pE0[0:HALO, :], AF.Exp)
        cx.act(s["expsh"][0:HALO, 512:1024], pE1[0:HALO, :], AF.Exp)
        cx.wait("tensor", "a")
        for hd in range(NH):
            pZ = pE0 if hd < 4 else pE1
            col = (hd % 4) * 128
            cx.mm(pZ[0:1, col:col + 128], s["onesbf"][:, 0:1],
                  s["exps"][:, hd * 128:(hd + 1) * 128], start=True, stop=False)
            cx.mm(pZ[0:1, col:col + 128], s["ohalbf"],
                  s["expsh"][:, hd * 128:(hd + 1) * 128], start=False, stop=True)
        cx.wait("vector", "t")
        cx.vop("reciprocal", s["rz"][0:1, 0:512], pE0[0:1, :])
        cx.vop("reciprocal", s["rz"][0:1, 512:1024], pE1[0:1, :])
        cx.wait("tensor", "v")
        cx.mm(pU[:, 0:512], s["e0f"], s["rz"][:, 0:512], start=True, stop=True)
        cx.mm(pU[:, 512:1024], s["e0f"], s["rz"][:, 512:1024],
              start=True, stop=True)
        cx.wait("vector", "t")
        cx.vop("tensor_tensor", s["exps"][:, 0:512], s["exps"][:, 0:512],
               pU[:, 0:512], op=ALU.mult)
        cx.vop("tensor_tensor", s["exps"][:, 512:1024], s["exps"][:, 512:1024],
               pU[:, 512:1024], op=ALU.mult)
        cx.vop("tensor_tensor", s["expsh"][0:HALO, 0:512], s["expsh"][0:HALO, 0:512],
               pU[0:HALO, 0:512], op=ALU.mult)
        cx.vop("tensor_tensor", s["expsh"][0:HALO, 512:1024], s["expsh"][0:HALO, 512:1024],
               pU[0:HALO, 512:1024], op=ALU.mult)
        cx.wait("tensor", "v")
        for hd in range(NH):
            cx.mm(pU[:, hd * 128:(hd + 1) * 128],
                  s["v"][:, j * D + hd * 128: j * D + (hd + 1) * 128],
                  s["exps"][:, hd * 128:(hd + 1) * 128], start=True, stop=False)
            cx.mm(pU[:, hd * 128:(hd + 1) * 128],
                  s["vhal"][:, hd * 128:(hd + 1) * 128],
                  s["expsh"][:, hd * 128:(hd + 1) * 128], start=False, stop=True)
        cx.wait("scalar", "t")
        for hd in range(NH):
            cx.act(s["yt"][:, hd * 128:(hd + 1) * 128], pU[:, hd * 128:(hd + 1) * 128],
                   AF.Copy)
        bk["vhal_a"].append(cx.cnt["a"])
        cx.wait("tensor", "a")
        for half, pX in ((0, pC), (1, pD)):
            for hd in range(NH):
                cx.mm(pX, s["yt"][:, hd * 128:(hd + 1) * 128],
                      s["wo"][:, hd * D + half * 512: hd * D + (half + 1) * 512],
                      start=(hd == 0), stop=(hd == 7))
        tp = gt % 2
        if gt >= 2:
            cx.wait_at("scalar", "v", bk["xattn_v"][gt - 2])
        cx.dma("scalar", s["xr"][:, tp * D:(tp + 1) * D],
               ap["x"][gt * 128:(gt + 1) * 128, :], "d_xr%d" % tp)
        cx.wait("vector", "t", "d_xr%d" % tp)
        if gt >= 2:
            psem, pval = bk["xa_d"][gt - 2]
            cx.wait_at("vector", psem, pval)
        cx.vop("tensor_tensor", s["xattn"][:, tp * D: tp * D + 512], pC,
               s["xr"][:, tp * D: tp * D + 512], op=ALU.add)
        cx.vop("tensor_tensor", s["xattn"][:, tp * D + 512:(tp + 1) * D], pD,
               s["xr"][:, tp * D + 512:(tp + 1) * D], op=ALU.add)
        bk["xattn_v"].append(cx.cnt["v"])
        cx.wait("scalar", "v")
        xasem = "d_xa%d" % tp
        cx.dma("scalar", ap["xad"][gt * 128:(gt + 1) * 128, :],
               s["xattn"][:, tp * D:(tp + 1) * D], xasem)
        bk["xa_d"].append((xasem, cx.cnt[xasem]))
        bk["tile_a"].append(cx.cnt["a"])

    def wait_group_out(cx, who, g):
        v_cnt, p_cnt = bk["group_out"][g]
        cx.wait_at(who, "d_mwo", v_cnt)
        cx.wait_at(who, "d_mwop", p_cnt)

    def emit_ffn(cx):
        attn_t = cx.cnt["t"]
        attn_v = cx.cnt["v"]
        attn_a = cx.cnt["a"]
        for sc in range(NFC):
            if sc == 0:
                cx.wait_at("sync", "t", attn_t)
            else:
                cx.wait_at("sync", "t", bk["xtf_war"])
            for hc in range(8):
                cx.dma("sync", s["xtf"][:, hc * FCH:(hc + 1) * FCH],
                       ap["xnd"][HALO + sc * FCH: HALO + (sc + 1) * FCH,
                                 hc * 128:(hc + 1) * 128], "d_xtf", transpose=True)
            for fb in range(8):
                if sc == 0:
                    emit_vector_builds(cx, 1, only_m1=True)
                bpar = fb % 2
                wait_group_out(cx, "sync", 4 * fb + 3)
                if sc == 0 and fb == 0:
                    cx.wait_at("sync", "t", attn_t)
                elif len(bk["m1_cons"]) >= 2:
                    cx.wait_at("sync", "t", bk["m1_cons"][-2])
                for hc in range(8):
                    cx.dma("sync", s["m1t"][:, bpar * 4096 + hc * FCH:
                                            bpar * 4096 + (hc + 1) * FCH],
                           ap["m1d"][fb * 512:(fb + 1) * 512, hc * 128:(hc + 1) * 128],
                           "d_m1%d" % bpar, transpose=True)
                cx.wait("tensor", "d_m1%d" % bpar, "d_xtf")
                if sc == 0 and fb == 0:
                    cx.wait_at("tensor", "v", bk["xattn_v"][-1])
                    cx.wait_at("tensor", "a", bk["tile_a"][-1])
                if fb == 0 and bk["out_v"]:
                    cx.wait_at("tensor", "v", bk["out_v"][-1])
                for fl in range(4):
                    ft = fb * 4 + fl
                    pX = (pA, pB, pC, pD)[ft % 4]
                    if len(bk["gelu_a"]) >= 4:
                        cx.wait_at("tensor", "a", bk["gelu_a"][-4])
                    for hc in range(8):
                        cx.mm(pX, s["m1t"][:, bpar * 4096 + hc * FCH + fl * 128:
                                           bpar * 4096 + hc * FCH + (fl + 1) * 128],
                              s["xtf"][:, hc * FCH:(hc + 1) * FCH],
                              start=(hc == 0), stop=(hc == 7))
                    cx.wait("scalar", "t")
                    cx.act(s["u"][:, ft * FCH:(ft + 1) * FCH], pX, AF.Gelu_apprx_tanh,
                           bias=s["mb1"][:, ft:ft + 1])
                    bk["gelu_a"].append(cx.cnt["a"])
                bk["m1_cons"].append(cx.cnt["t"])
            bk["xtf_war"] = cx.cnt["t"]
            if sc == 0:
                emit_vector_builds(cx, 64)
            banksets = [(pA, pB), (pC, pD), (pE0, pE1),
                        (pU[:, 0:512], pU[:, 512:1024])]
            for ft in range(32):
                mpar = ft % 2
                wait_group_out(cx, "sync", 60 + ft // 8)
                if len(bk["m2_cons"]) >= 2:
                    cx.wait_at("sync", "t", bk["m2_cons"][-2])
                cx.dma("sync", s["m2t"][:, mpar * D:(mpar + 1) * D],
                       ap["m2d"][:, ft * 128:(ft + 1) * 128], "d_m2%d" % mpar,
                       transpose=True)
                cx.wait("tensor", "d_m2%d" % mpar)
                if ft == 0:
                    cx.wait_at("tensor", "a", bk["gelu_a"][-1])
                    cx.wait_at("tensor", "v", bk["xattn_v"][-1])
                    cx.wait_at("tensor", "v", bk["m2row_v"])
                    for ss in range(4):
                        p0, p1 = banksets[ss]
                        cx.mm(p0, s["e0bf"], s["m2rowb"][:, 0:512],
                              start=True, stop=False)
                        cx.mm(p1, s["e0bf"], s["m2rowb"][:, 512:1024],
                              start=True, stop=False)
                for ss in range(4):
                    p0, p1 = banksets[ss]
                    cx.mm(p0, s["u"][:, ft * FCH + ss * 128: ft * FCH + (ss + 1) * 128],
                          s["m2t"][:, mpar * D: mpar * D + 512],
                          start=False, stop=(ft == 31))
                    cx.mm(p1, s["u"][:, ft * FCH + ss * 128: ft * FCH + (ss + 1) * 128],
                          s["m2t"][:, mpar * D + 512:(mpar + 1) * D],
                          start=False, stop=(ft == 31))
                bk["m2_cons"].append(cx.cnt["t"])
            for ss in range(4):
                gt = sc * 4 + ss
                p0, p1 = banksets[ss]
                tp = gt % 2
                xasem, xaval = bk["xa_d"][gt]
                cx.wait_at("scalar", xasem, xaval)
                if gt < 2:
                    cx.wait_at("scalar", "v", bk["xattn_v"][-1])
                if gt >= 2:
                    cx.wait_at("scalar", "v", bk["out_v"][gt - 2])
                cx.dma("scalar", s["xf"][:, tp * D:(tp + 1) * D],
                       ap["xad"][gt * 128:(gt + 1) * 128, :], "d_xf%d" % tp)
                cx.wait("vector", "t", "d_xf%d" % tp)
                if gt >= 2:
                    psem, pval = bk["out_d"][gt - 2]
                    cx.wait_at("vector", psem, pval)
                cx.vop("tensor_tensor", s["outb"][:, tp * D: tp * D + 512], p0,
                       s["xf"][:, tp * D: tp * D + 512], op=ALU.add)
                cx.vop("tensor_tensor", s["outb"][:, tp * D + 512:(tp + 1) * D], p1,
                       s["xf"][:, tp * D + 512:(tp + 1) * D], op=ALU.add)
                bk["out_v"].append(cx.cnt["v"])
                cx.wait("scalar", "v")
                osem = "d_out%d" % tp
                cx.dma("scalar", ap["out"][gt * 128:(gt + 1) * 128, :],
                       s["outb"][:, tp * D:(tp + 1) * D], osem)
                bk["out_d"].append((osem, cx.cnt[osem]))

    PHASES = int(os.environ.get("K_PHASES", "99"))

    def emit_dummy_out(cx):
        dump = os.environ.get("K_DUMP", "")
        if dump == "xn":
            cx.wait("gpsimd", "d_xw0", "d_xw1")
            cx.dma("gpsimd", ap["out"], ap["xnd"][HALO:HALO + OWN, :], "d_out0")
            cx.wait("gpsimd", "d_out0")
            return
        if dump == "xt":
            cx.wait("gpsimd", "d_xt", "t")
            for r in range(8):
                cx.dma("gpsimd", ap["out"][r * 128:(r + 1) * 128, 0:HS],
                       s["xt"][:, r * HS:(r + 1) * HS], "d_out0")
            cx.wait("gpsimd", "d_out0")
            return
        if dump == "gb1r":
            cx.wait("gpsimd", "d_ldg", "d_ld")
            cx.dma("gpsimd", ap["out"][0:1, 0:D], s["gb1r"], "d_out0")
            cx.wait("gpsimd", "d_out0")
            return
        if dump == "gw1t":
            cx.wait("gpsimd", "d_ldg", "d_ld")
            for r in range(8):
                cx.dma("gpsimd", ap["out"][r * 128:(r + 1) * 128, 0:1024],
                       s["gw1t"][:, r * 1024:(r + 1) * 1024], "d_out0")
            cx.dma("gpsimd", ap["out"][1024:1152, 0:1024],
                   s["wq"][:, 0:1024], "d_out0")
            cx.wait("gpsimd", "d_out0")
            return
        if dump == "hrelu":
            cx.wait("gpsimd", "a", "v", "t")
            cx.dma("gpsimd", ap["out"][0:128, 0:D], s["hrelu"], "d_out0")
            cx.dma("gpsimd", ap["out"][128:256, 0:8], s["hbacc"], "d_out0")
            cx.dma("gpsimd", ap["out"][256:384, 0:256], s["junk1"][:, 0:256], "d_out0")
            cx.dma("gpsimd", ap["out"][384:512, 0:128], s["onesbf"], "d_out0")
            cx.dma("gpsimd", ap["out"][512:640, 0:128], s["e0bf"], "d_out0")
            cx.dma("gpsimd", ap["out"][640:768, 0:128], s["identbf"], "d_out0")
            cx.dma("gpsimd", ap["out"][768:896, 0:128], s["onesf"], "d_out0")
            cx.dma("gpsimd", ap["out"][896:1024, 0:128], s["e0f"], "d_out0")
            cx.dma("gpsimd", ap["out"][1024:1152, 0:32], s["idhal"], "d_out0")
            cx.dma("gpsimd", ap["out"][1280:1408, 0:32], s["selbf"], "d_out0")
            cx.wait("gpsimd", "d_out0")
            return
        if dump == "xad":
            cx.wait("gpsimd", "d_xa0", "d_xa1")
            cx.dma("gpsimd", ap["out"], ap["xad"], "d_out0")
            cx.wait("gpsimd", "d_out0")
            return
        if dump == "hbar":
            cx.wait("gpsimd", "a", "v", "t")
            cx.dma("gpsimd", ap["out"][0:128, 0:8], s["hbarT"], "d_out0")
            cx.dma("gpsimd", ap["out"][128:129, 0:8], s["grow"][0:1, :], "d_out0")
            cx.dma("gpsimd", ap["out"][129:257, 0:8], s["gbc"], "d_out0")
            cx.dma("gpsimd", ap["out"][257:385, 0:32], s["mb1"], "d_out0")
            cx.wait("gpsimd", "d_out0")
            return
        for gt in range(16):
            cx.wait("vector", "a", "t")
            cx.vop("memset", s["outb"][:, 0:D], 0.0)
            cx.wait("scalar", "v")
            cx.dma("scalar", ap["out"][gt * 128:(gt + 1) * 128, :],
                   s["outb"][:, 0:D], "d_out0")
        cx.wait("scalar", "d_out0")

    def full_schedule(cx):
        emit_p0(cx)
        if PHASES >= 2:
            emit_ln_gate(cx)
        bk["pa_end_v"] = cx.cnt["v"]
        bk["pa_end_a"] = cx.cnt["a"]
        if PHASES >= 3:
            emit_g(cx)
        if PHASES >= 4:
            emit_attn(cx)
        if PHASES >= 5:
            emit_ffn(cx)
            cx.wait("scalar", "d_out0", "d_out1")
        else:
            emit_dummy_out(cx)

    with nc.Block() as block:
        def run_for(name):
            def body(eng):
                bk.clear()
                bk.update({
                    "gw1ev": [], "xband_v": [], "xband_a": [], "xw_d": [],
                    "relu_a": [], "hbar_t": [], "xt_gate_t": [],
                    "group_out": {},
                    "bld_n": 0, "bld_idx": 0, "slot_dma": {}, "mac_cnt": {},
                    "acc_done": {0: 0, 1: 0}, "out_sem": {0: None, 1: None},
                    "vq": 0, "hraw_a": [], "qraw_a": [], "kstt_v": [], "ppa": [],
                    "hbacc_v": [],
                    "qt_war": 0, "kt_war": 0, "xt_war": 0,
                    "vhal_a": [], "xattn_v": [], "xa_d": [], "tile_a": [],
                    "m1_cons": [], "m2_cons": [], "gelu_a": [], "xtf_war": 0,
                    "out_v": [], "out_d": [],
                })
                c.reset(name, eng)
                full_schedule(c)
            return body
        block.sync(run_for("sync"))
        block.tensor(run_for("tensor"))
        block.vector(run_for("vector"))
        block.scalar(run_for("scalar"))
        block.gpsimd(run_for("gpsimd"))

    stack.close()
    return nc


_NC_CACHE = {}

def _consts():
    ident = np.eye(128)
    onesm = np.ones((128, 128))
    sel = np.zeros((128, HALO))
    for p in range(HALO):
        sel[128 - HALO + p, p] = 1.0
    slopes = np.array([0.5 ** i for i in range(NH)])
    q_idx = np.arange(128)
    rel_m = np.arange(128)[:, None] - q_idx[None, :]
    valid_m = (rel_m >= -W_ATT) & (rel_m <= 0)
    rel_h = (np.arange(HALO)[:, None] - HALO) - q_idx[None, :]
    valid_h = (rel_h >= -W_ATT) & (rel_h <= 0)
    biasm = np.empty((128, NH * 128), dtype=np.float32)
    bhal = np.empty((HALO, NH * 128), dtype=np.float32)
    for hd in range(NH):
        biasm[:, hd * 128:(hd + 1) * 128] = np.where(valid_m, slopes[hd] * rel_m, NEG)
        bhal[:, hd * 128:(hd + 1) * 128] = np.where(valid_h, slopes[hd] * rel_h, NEG)
    e0 = np.zeros((128, 128)); e0[0, :] = 1.0
    idh = np.zeros((128, HALO)); idh[np.arange(HALO), np.arange(HALO)] = 1.0
    ohal = np.zeros((128, 1)); ohal[0:HALO, 0] = 1.0
    bhal_p = np.zeros((128, NH * 128), dtype=np.float32); bhal_p[0:HALO] = bhal
    bhal0_p = np.zeros((128, NH * 128), dtype=np.float32); bhal0_p[0:HALO] = NEG
    return {
        "identbf": ident.astype(BF), "identf": ident.astype(np.float32),
        "onesbf": onesm.astype(BF), "onesf": onesm.astype(np.float32),
        "e0bf": e0.astype(BF), "e0f": e0.astype(np.float32),
        "idhal": idh.astype(BF), "ohalbf": ohal.astype(BF),
        "selbf": sel.astype(BF),
        "biasm": biasm.astype(BF), "bhal": bhal_p.astype(BF),
        "bhal0_neg": bhal0_p.astype(BF),
    }


def kernel(**inputs):
    cst = _consts()
    if "nc" not in _NC_CACHE:
        _NC_CACHE["nc"] = build_program()
    nc = _NC_CACHE["nc"]

    f = lambda k: np.ascontiguousarray(np.asarray(inputs[k], dtype=np.float32))
    x = f("L0_attn_x")
    Tb1 = f("L0_ffn_Tb1")
    Tb2 = f("L0_ffn_Tb2")
    Tb1c = np.ascontiguousarray(
        Tb1.reshape(8, 32, 128).transpose(2, 0, 1).reshape(128, 8 * 32))
    Tb2c = np.ascontiguousarray(
        Tb2.reshape(8, 8, 128).transpose(2, 0, 1).reshape(128, 8 * 8))
    gw1t = np.ascontiguousarray(
        f("L0_ffn_gate_w1").T.reshape(8, 128, D).transpose(1, 0, 2).reshape(128, 8 * D))
    gw2t = np.ascontiguousarray(
        f("L0_ffn_gate_w2").T.reshape(8, 128, E).transpose(1, 0, 2).reshape(128, 8 * E))
    shared = {
        "wq": f("L0_attn_wq"), "wk": f("L0_attn_wk"), "wv": f("L0_attn_wv"),
        "wo": f("L0_attn_wo"),
        "qwc": np.ascontiguousarray(f("L0_attn_q_norm_w").reshape(8, 128).T),
        "kwc": np.ascontiguousarray(f("L0_attn_k_norm_w").reshape(8, 128).T),
        "gw1tc": gw1t, "gw2tc": gw2t,
        "gb1": np.ascontiguousarray(f("L0_ffn_gate_b1").reshape(1, D)),
        "gb2r": np.ascontiguousarray(f("L0_ffn_gate_b2").reshape(1, E)),
        "W1": f("L0_ffn_W1"), "W2": f("L0_ffn_W2"),
        "TW1": f("L0_ffn_TW1"), "TW2": f("L0_ffn_TW2"),
        "b1c": np.ascontiguousarray(f("L0_ffn_b1").reshape(32, 128).T),
        "b2c": np.ascontiguousarray(f("L0_ffn_b2").reshape(8, 128).T),
        "Tb1c": Tb1c, "Tb2c": Tb2c,
        "identbf": cst["identbf"],
        "onesbf": cst["onesbf"], "onesf": cst["onesf"], "selbf": cst["selbf"],
        "e0bf": cst["e0bf"], "e0f": cst["e0f"],
        "idhal": cst["idhal"], "ohalbf": cst["ohalbf"],
        "biasm": cst["biasm"], "bhal": cst["bhal"],
        "epsc": np.tile(np.array([[1e-5, 1e-6]], dtype=np.float32), (128, 1)),
        "epsc3": np.full((128, 1), -0.5 * math.log(HD), dtype=np.float32),
    }
    in_maps = []
    for core in range(8):
        b, t = core // 2, core % 2
        m = dict(shared)
        xb = x[b]
        m["x"] = np.ascontiguousarray(np.concatenate([xb[t * OWN:], xb[:t * OWN]], axis=0))
        m["xhalo"] = (np.zeros((HALO, D), dtype=np.float32) if t == 0
                      else np.ascontiguousarray(xb[OWN - HALO: OWN]))
        m["bhal0"] = cst["bhal0_neg"] if t == 0 else cst["bhal"]
        in_maps.append(m)

    res = run_bass_kernel_spmd(nc, in_maps, core_ids=list(range(8)))
    out = np.empty((B, T, D), dtype=np.float32)
    for core in range(8):
        b, t = core // 2, core % 2
        out[b, t * OWN:(t + 1) * OWN] = np.asarray(res.results[core]["out"])
    return out


# version 18
# speedup vs baseline: 1.0298x; 1.0298x over previous
import math
import os
import numpy as np
import ml_dtypes
from contextlib import ExitStack

import concourse.bass as bass
import concourse.mybir as mybir
from concourse.bass_utils import run_bass_kernel_spmd

F32 = mybir.dt.float32
BF16 = mybir.dt.bfloat16
AF = mybir.ActivationFunctionType
ALU = mybir.AluOpType
BF = ml_dtypes.bfloat16

B, T, D = 4, 4096, 1024
OWN = 2048
NH, HD = 8, 128
FF = 4096
E = 8
NEG = -1e9
W_ATT = 16
HALO = 32
SC = 512
NCH = OWN // SC
NT = 4
FCH = 512
NFC = OWN // FCH
HS = HALO + SC

SEM_NAMES = [
    "d_ld", "d_ldg", "d_x0", "d_x1", "d_xw0", "d_xw1", "d_xt", "d_xtf",
    "d_tw", "d_twp", "d_mwo", "d_mwop", "d_m10", "d_m11", "d_m20", "d_m21",
    "d_xr0", "d_xr1", "d_xa0", "d_xa1", "d_xf0", "d_xf1", "d_out0", "d_out1",
    "d_mb", "d_ts0", "d_ts1", "d_ts2", "d_ts3",
    "t", "v", "a", "p",
]


class Ctx:
    def __init__(self, sems):
        self.sems = sems
        self.eng_name = None
        self.eng = None
        self.cnt = {k: 0 for k in sems}

    def reset(self, name, eng):
        self.eng_name = name
        self.eng = eng
        for k in self.cnt:
            self.cnt[k] = 0

    def on(self, name):
        return self.eng_name == name

    def inc(self, sname, instr, k=1):
        if instr is not None:
            instr.then_inc(self.sems[sname], k)
        self.cnt[sname] += k

    def wait(self, who, *ss):
        if self.on(who):
            for sname in ss:
                if self.cnt[sname] > 0:
                    self.eng.wait_ge(self.sems[sname], self.cnt[sname])

    def wait_at(self, who, sname, val):
        if self.on(who) and val > 0:
            self.eng.wait_ge(self.sems[sname], val)

    def mm(self, out, lhsT, rhs, start, stop, skip=False):
        if self.on("tensor"):
            i = self.eng.matmul(out, lhsT, rhs, start=start, stop=stop,
                                skip_group_check=skip)
        else:
            i = None
        self.inc("t", i)

    def trans(self, out, in_, ident):
        i = self.eng.transpose(out, in_, ident) if self.on("tensor") else None
        self.inc("t", i)

    def act(self, out, in_, func, **kw):
        i = self.eng.activation(out, in_, func, **kw) if self.on("scalar") else None
        self.inc("a", i)

    def vop(self, name, *args, who="vector", **kw):
        sem = "v" if who == "vector" else "p"
        i = getattr(self.eng, name)(*args, **kw) if self.on(who) else None
        self.inc(sem, i)

    def dma(self, q, out, in_, sem, transpose=False):
        if self.on(q):
            i = self.eng.dma_start(out=out, in_=in_, transpose=transpose)
        else:
            i = None
        self.inc(sem, i, 16)


def build_program():
    nc = bass.Bass("TRN2", target_bir_lowering=False, dynamic_dma_scratch_size=8192)

    ap = {}
    def din(name, shape, dtype=F32):
        ap[name] = nc.dram_tensor(name, list(shape), dtype, kind="ExternalInput").ap()
    din("x", (T, D))
    din("xhalo", (HALO, D))
    din("wq", (D, D)); din("wk", (D, D)); din("wv", (D, D)); din("wo", (D, D))
    din("qwc", (128, 8)); din("kwc", (128, 8))
    din("gw1tc", (128, 8 * D)); din("gw2tc", (128, 8 * E))
    din("gb1", (1, D)); din("gb2r", (1, E))
    din("W1", (FF, D)); din("W2", (D, FF))
    din("TW1", (E, FF, D)); din("TW2", (E, D, FF))
    din("b1c", (128, 32)); din("b2c", (128, 8))
    din("Tb1c", (128, 8 * 32)); din("Tb2c", (128, 8 * 8))
    din("identbf", (128, 128), BF16)
    din("onesbf", (128, 128), BF16); din("onesf", (128, 128), F32)
    din("e0bf", (128, 128), BF16); din("e0f", (128, 128), F32)
    din("idhal", (128, HALO), BF16); din("ohalbf", (128, 1), BF16)
    din("selbf", (128, HALO), BF16)
    din("biasm", (128, NH * 128), BF16)
    din("epsc", (128, 2))
    din("epsc3", (128, 1))
    din("bhal", (128, NH * 128), BF16)
    din("bhal0", (128, NH * 128), BF16)
    ap["xnd"] = nc.dram_tensor("xnd", [HALO + T, D], BF16, kind="Internal").ap()
    ap["m1d"] = nc.dram_tensor("m1d", [FF, D], BF16, kind="Internal").ap()
    ap["m2d"] = nc.dram_tensor("m2d", [D, FF], BF16, kind="Internal").ap()
    ap["xad"] = nc.dram_tensor("xad", [OWN, D], F32, kind="Internal").ap()
    ap["mbd"] = nc.dram_tensor("mbd", [1, D], F32, kind="Internal").ap()
    ap["out"] = nc.dram_tensor("out", [OWN, D], F32, kind="ExternalOutput").ap()

    stack = ExitStack()
    sbuf = lambda name, shape, dtype: stack.enter_context(
        nc.sbuf_tensor(name + "_s", list(shape), dtype))[:]
    psum = lambda name, shape: stack.enter_context(
        nc.psum_tensor(name, list(shape), F32))[:]

    s = {}
    for n, sh, dt in [
        ("identbf", (128, 128), BF16),
        ("onesbf", (128, 128), BF16), ("onesf", (128, 128), F32),
        ("e0bf", (128, 128), BF16), ("e0f", (128, 128), F32),
        ("idhal", (128, HALO), BF16), ("ohalbf", (128, 1), BF16),
        ("selbf", (128, HALO), BF16), ("biasm", (128, NH * 128), BF16),
        ("bhal", (128, NH * 128), BF16), ("bhal0", (128, NH * 128), BF16),
        ("gw2t", (128, 8 * E), BF16), ("qwc", (128, 8), F32), ("kwc", (128, 8), F32),
        ("gb1r", (128, D), BF16), ("gb2r", (1, E), F32),
        ("b1c", (128, 32), F32), ("b2c", (128, 8), F32),
        ("Tb1c", (128, 8 * 32), F32), ("Tb2c", (128, 8 * 8), F32),
        ("grow", (128, E), F32), ("m2row", (1, D), F32), ("m2rowb", (128, D), BF16),
        ("rrows", (128, 1024), F32),
        ("gbc", (128, E), F32), ("mb1", (128, 32), F32), ("mb2", (128, 8), F32),
        ("hbarT", (128, 8), BF16), ("hbacc", (128, 8), F32),
        ("stats", (128, 12), F32), ("xnb", (128, 2 * D), BF16),
        ("junk1", (128, D), F32),
        ("hrelu", (128, D), BF16),
        ("xt", (128, 8 * HS), BF16),
        ("raw", (128, 8 * SC), F32), ("hraw", (128, 8 * HALO), F32),
        ("sq", (128, 2 * SC), BF16), ("sqh", (128, HALO), BF16),
        ("rbq", (128, SC), F32), ("rbk", (128, SC), F32), ("rbh", (128, HALO), F32),
        ("rz", (128, NH * 128), F32),
        ("vhal", (128, D), BF16),
        ("exps", (128, NH * 128), BF16), ("expsh", (128, NH * 128), BF16),
        ("yt", (128, NH * 128), BF16),
        ("accv", (128, D), F32), ("mwov", (128, D), BF16),
        ("accv2", (128, D), F32), ("mwov2", (128, D), BF16),
    ]:
        s[n] = sbuf(n, sh, dt)
    s["tmp1"] = s["junk1"][0:1, 0:SC]
    s["rqrow"] = s["rrows"][0:1, 0:SC]
    s["rkrow"] = s["rrows"][0:1, SC:2 * SC]
    s["epsc"] = sbuf("epsc", (128, 2), F32)
    nc.const_aps.register(F32, 1e-5, s["epsc"][:, 0:1])
    nc.const_aps.register(F32, 1e-6, s["epsc"][:, 1:2])
    LNB = -0.5 * math.log(HD)
    s["epsc3"] = sbuf("epsc3", (128, 1), F32)
    nc.const_aps.register(F32, LNB, s["epsc3"][:, 0:1])

    O1 = sbuf("O1", (128, 32768), BF16)
    s["wq"] = O1[:, 0:8192]; s["wk"] = O1[:, 8192:16384]
    s["wv"] = O1[:, 16384:24576]; s["wo"] = O1[:, 24576:32768]
    s["u"] = O1[:, 0:32 * FCH]
    s["m1t"] = O1[:, 16384:24576]
    s["xtf"] = O1[:, 24576:24576 + 8 * FCH]
    s["m2t"] = O1[:, 28672:28672 + 2 * D]
    O2 = sbuf("O2", (128, 8192), BF16)
    s["xband"] = O2.bitcast(F32)
    s["twv"] = O2.bitcast(F32)

    O5 = sbuf("O5", (128, 8704), BF16)
    s["gw1t"] = O5[:, 0:8192]
    s["qt"] = O5[:, 0:8 * SC]
    s["kt"] = O5[:, 4096:4096 + 8 * HS]
    O6 = sbuf("O6", (128, 2 * D), F32)
    s["xr"] = O6
    s["xf"] = O6
    O7 = sbuf("O7", (128, 2 * D), F32)
    s["xattn"] = O7
    O8 = sbuf("O8", (128, 2 * D), F32)
    s["v"] = O8.bitcast(BF16)
    s["outb"] = O8

    pA = psum("pA", (128, 512))
    pB = psum("pB", (128, 512))
    pC = psum("pC", (128, 512))
    pD = psum("pD", (128, 512))
    pE0 = psum("pE0", (128, 512))
    pE1 = psum("pE1", (128, 512))
    pU = psum("pU", (128, 1024))

    sems = {n: stack.enter_context(nc.semaphore(n)) for n in SEM_NAMES}
    c = Ctx(sems)
    bk = {}

    def W(name, kt, hd):
        return s[name][:, kt * D + hd * 128: kt * D + (hd + 1) * 128]

    P0SUB = int(os.environ.get("K_P0SUB", "99"))

    def emit_p0(cx):
        if P0SUB < 1:
            return
        for n in ("identbf", "onesbf", "onesf", "selbf", "e0bf", "e0f",
                  "idhal", "ohalbf",
                  "biasm", "bhal", "bhal0", "gb2r", "Tb1c", "Tb2c", "epsc", "epsc3",
                  "qwc", "kwc", "b1c", "b2c"):
            cx.dma("sync", s[n], ap[n], "d_ld")
        if P0SUB == 1:
            return
        for n in ("wq", "wk", "wv", "wo"):
            cx.dma("gpsimd", s[n].rearrange("p (t d) -> p t d", d=D),
                   ap[n].rearrange("(t p) d -> p t d", p=128), "d_ldg")
        cx.dma("gpsimd", s["gw1t"].rearrange("p (t d) -> p t d", d=D),
               ap["gw1tc"].rearrange("p (t d) -> p t d", d=D), "d_ldg")
        cx.dma("gpsimd", s["gw2t"], ap["gw2tc"], "d_ldg")
        for nm in ("vhal", "gb1r", "rrows", "grow", "rz", "m2rowb", "expsh"):
            cx.vop("memset", s[nm], 0.0)
        bk["p0ms_v"] = cx.cnt["v"]
        cx.wait_at("gpsimd", "v", bk["p0ms_v"])
        cx.dma("gpsimd", s["gb1r"][0:1, :], ap["gb1"], "d_ldg")
        if P0SUB < 40:
            return
        cx.dma("sync", s["raw"][0:HALO, 1024:2048], ap["xhalo"], "d_mb")
        src = s["raw"][0:HALO, 1024:2048]
        jout = s["raw"][0:HALO, 3072:4096]
        scol = 6
        cx.wait("vector", "d_ld", "d_mb")
        cx.vop("tensor_scalar", jout, src, 1.0 / D, 0.0,
               op0=ALU.mult, op1=ALU.add,
               accum_out=s["stats"][0:HALO, scol:scol + 1])
        cx.wait("scalar", "v", "d_ld", "d_mb")
        cx.act(jout, src, AF.Square, scale=1.0 / 32.0,
               accum_out=s["stats"][0:HALO, scol + 1:scol + 2])
        cx.wait("vector", "a")
        cx.vop("scalar_tensor_tensor", s["stats"][0:HALO, scol + 2:scol + 3],
               s["stats"][0:HALO, scol:scol + 1], s["stats"][0:HALO, scol:scol + 1],
               s["stats"][0:HALO, scol + 1:scol + 2], op0=ALU.mult, op1=ALU.subtract)
        cx.wait("scalar", "v")
        cx.act(s["stats"][0:HALO, scol + 3:scol + 4], s["stats"][0:HALO, scol + 2:scol + 3],
               AF.Ln, scale=-1.0, bias=1e-5)
        cx.wait("scalar", "a")
        cx.act(s["stats"][0:HALO, scol + 4:scol + 5], s["stats"][0:HALO, scol + 3:scol + 4],
               AF.Exp, scale=-0.5)
        cx.wait("vector", "a")
        cx.vop("tensor_scalar", s["xnb"][0:HALO, D:2 * D], src,
               s["stats"][0:HALO, scol:scol + 1], s["stats"][0:HALO, scol + 4:scol + 5],
               op0=ALU.subtract, op1=ALU.mult)
        cx.wait("scalar", "v")
        cx.dma("scalar", ap["xnd"][0:HALO, :], s["xnb"][0:HALO, D:2 * D], "d_xw1")
        bk["xw_d"].append(("d_xw1", cx.cnt["d_xw1"]))

    def emit_ln_tile(cx, tile, gt, bpar):
        scol = 6 * (gt % 2)
        dx = "d_x%d" % bpar
        cx.wait("vector", dx, "a")
        cx.vop("tensor_scalar", s["junk1"], tile, 1.0 / D, 0.0,
               op0=ALU.mult, op1=ALU.add, accum_out=s["stats"][:, scol:scol + 1])
        cx.wait("scalar", "v", dx, "d_ld")
        cx.act(s["junk1"], tile, AF.Square, scale=1.0 / 32.0,
               accum_out=s["stats"][:, scol + 1:scol + 2])
        cx.wait("vector", "a")
        cx.vop("scalar_tensor_tensor", s["stats"][:, scol + 2:scol + 3],
               s["stats"][:, scol:scol + 1], s["stats"][:, scol:scol + 1],
               s["stats"][:, scol + 1:scol + 2], op0=ALU.mult, op1=ALU.subtract)
        cx.wait("scalar", "v")
        cx.act(s["stats"][:, scol + 3:scol + 4], s["stats"][:, scol + 2:scol + 3],
               AF.Ln, scale=-1.0, bias=1e-5)
        cx.wait("scalar", "a")
        cx.act(s["stats"][:, scol + 4:scol + 5], s["stats"][:, scol + 3:scol + 4],
               AF.Exp, scale=-0.5)
        cx.wait("vector", "a")
        if len(bk["xw_d"]) >= 2:
            psem, pval = bk["xw_d"][-2]
            cx.wait_at("vector", psem, pval)
        cx.vop("tensor_scalar", s["xnb"][:, (gt % 2) * D:((gt % 2) + 1) * D], tile,
               s["stats"][:, scol:scol + 1], s["stats"][:, scol + 4:scol + 5],
               op0=ALU.subtract, op1=ALU.mult)
        cx.wait("scalar", "v")
        xwsem = "d_xw%d" % (gt % 2)
        cx.dma("scalar", ap["xnd"][HALO + gt * 128: HALO + (gt + 1) * 128, :],
               s["xnb"][:, (gt % 2) * D:((gt % 2) + 1) * D], xwsem)
        bk["xw_d"].append((xwsem, cx.cnt[xwsem]))

    def emit_ln_gate(cx):
        for gb in range(int(os.environ.get("K_STOPGB", "8"))):
            for half in range(2):
                band = gb * 2 + half
                bpar = band % 2
                if band >= 2:
                    cx.wait_at("sync", "v", bk["xband_v"][band - 2])
                    cx.wait_at("sync", "a", bk["xband_a"][band - 2])
                cx.dma("sync",
                       s["xband"][:, bpar * 2048:(bpar + 1) * 2048].rearrange(
                           "p (t d) -> p t d", d=D),
                       ap["x"][band * 256:(band + 1) * 256, :].rearrange(
                           "(t p) d -> p t d", p=128), "d_x%d" % bpar)
                for st in range(2):
                    tile = s["xband"][:, bpar * 2048 + st * D: bpar * 2048 + (st + 1) * D]
                    emit_ln_tile(cx, tile, band * 2 + st, bpar)
                bk["xband_v"].append(cx.cnt["v"])
                bk["xband_a"].append(cx.cnt["a"])
            if gb >= 1:
                cx.wait_at("sync", "t", bk["xt_gate_t"][-1])
            cx.wait("sync", "d_xw0", "d_xw1")
            for hc in range(8):
                cx.dma("sync", s["xt"][:, hc * HS:(hc + 1) * HS],
                       ap["xnd"][gb * SC: gb * SC + HS, hc * 128:(hc + 1) * 128],
                       "d_xt", transpose=True)
            for st in range(4):
                cx.wait("tensor", "d_xt", "d_ld", "d_ldg")
                cx.wait_at("tensor", "v", bk["p0ms_v"])
                if gb or st:
                    cx.wait_at("tensor", "a", bk["relu_a"][-1])
                KG = os.environ.get("K_GATE", "")
                for half, pX in ((0, pC), (1, pD)):
                    if KG != "nobias":
                        cx.mm(pX, s["e0bf"],
                              s["gb1r"][:, half * 512:(half + 1) * 512],
                              start=True, stop=False)
                    for kt in range(8):
                        rhs = (s["gw1t"][:, 0:512] if KG == "fixrhs" else
                               s["gw1t"][:, kt * D + half * 512: kt * D + (half + 1) * 512])
                        lhs = (s["xt"][:, HALO:HALO + 128] if KG == "fixlhs" else
                               s["xt"][:, kt * HS + HALO + st * 128:
                                       kt * HS + HALO + (st + 1) * 128])
                        cx.mm(pX, lhs, rhs,
                              start=(kt == 0 and KG == "nobias"), stop=(kt == 7))
                cx.wait("scalar", "t")
                if bk["hbar_t"]:
                    cx.wait_at("scalar", "t", bk["hbar_t"][-1])
                cx.act(s["hrelu"][:, 0:512], pC, AF.Relu)
                cx.act(s["hrelu"][:, 512:1024], pD, AF.Relu)
                bk["relu_a"].append(cx.cnt["a"])
                cx.wait("tensor", "a")
                tj = gb * 4 + st
                if os.environ.get("K_NOHBAR", "") != "1":
                    for cb in range(8):
                        cx.mm(pE0[:, cb * 32 + tj: cb * 32 + tj + 1],
                              s["hrelu"][:, cb * 128:(cb + 1) * 128],
                              s["onesbf"][:, 0:1], start=True, stop=True)
                bk["hbar_t"].append(cx.cnt["t"])
            bk["xt_gate_t"].append(cx.cnt["t"])
        for _ in range(2):
            cx.mm(pE0[:, 300:301], s["onesbf"], s["onesbf"][:, 0:1],
                  start=True, stop=True)
        cx.wait("vector", "t")
        for cb in range(8):
            cx.vop("tensor_scalar", s["junk1"][:, cb * 32:(cb + 1) * 32],
                   pE0[:, cb * 32:(cb + 1) * 32], 1.0, 0.0,
                   op0=ALU.mult, op1=ALU.add,
                   accum_out=s["hbacc"][:, cb:cb + 1])
        bk["hbred_v"] = cx.cnt["v"]

    def emit_g(cx):
        cx.wait("scalar", "t", "v")
        cx.act(s["hbarT"], s["hbacc"], AF.Identity, scale=1.0 / T)
        cx.wait("tensor", "a")
        for ot in range(8):
            cx.mm(pA[0:1, 0:E], s["hbarT"][:, ot:ot + 1],
                  s["gw2t"][:, ot * E:(ot + 1) * E], start=(ot == 0), stop=(ot == 7))
        cx.wait("vector", "t")
        cx.vop("tensor_tensor", s["grow"][0:1, :], pA[0:1, 0:E], s["gb2r"], op=ALU.add)
        cx.wait("tensor", "v")
        cx.mm(pU[:, 64:64 + E], s["e0f"], s["grow"], start=True, stop=True)
        cx.wait("scalar", "t")
        cx.act(s["gbc"], pU[:, 64:64 + E], AF.Copy)
        bk["gbc_a"] = cx.cnt["a"]
        cx.wait("vector", "a")
        cx.vop("tensor_copy", s["mb1"], s["b1c"])
        cx.vop("tensor_copy", s["mb2"], s["b2c"])
        for e in range(E):
            cx.wait("vector", "v")
            cx.vop("scalar_tensor_tensor", s["mb1"], s["Tb1c"][:, e * 32:(e + 1) * 32],
                   s["gbc"][:, e:e + 1], s["mb1"], op0=ALU.mult, op1=ALU.add)
            cx.wait("vector", "v")
            cx.vop("scalar_tensor_tensor", s["mb2"], s["Tb2c"][:, e * 8:(e + 1) * 8],
                   s["gbc"][:, e:e + 1], s["mb2"], op0=ALU.mult, op1=ALU.add)
        cx.wait("scalar", "v")
        for t in range(8):
            cx.dma("scalar", ap["mbd"][0:1, t * 128:(t + 1) * 128].rearrange("a p -> p a"),
                   s["mb2"][:, t:t + 1], "d_mb")
        cx.wait("sync", "d_mb")
        cx.dma("sync", s["m2row"], ap["mbd"], "d_mb")
        cx.wait("vector", "d_mb")
        cx.wait("vector", "v")
        cx.vop("tensor_copy", s["m2rowb"][0:1, :], s["m2row"])
        bk["m2row_v"] = cx.cnt["v"]
        bk["g_done_t"] = cx.cnt["t"]

    BLD_ORDER = list(range(32)) + [32 + ht * 4 + fq
                                   for fq in range(4) for ht in range(8)]

    def emit_build_group(cx, g):
        m1 = g < 32
        q = "sync" if m1 else "scalar"
        if m1:
            ft = g
            wsrc = ap["W1"][ft * 128:(ft + 1) * 128, :]
            twsrc = ap["TW1"][:, ft * 128:(ft + 1) * 128, :]
            dst = ap["m1d"][ft * 128:(ft + 1) * 128, :]
        else:
            ht, fq = (g - 32) // 4, (g - 32) % 4
            wsrc = ap["W2"][ht * 128:(ht + 1) * 128, fq * D:(fq + 1) * D]
            twsrc = ap["TW2"][:, ht * 128:(ht + 1) * 128, fq * D:(fq + 1) * D]
            dst = ap["m2d"][ht * 128:(ht + 1) * 128, fq * D:(fq + 1) * D]
        par = bk["bld_n"] % 2
        acc = s["accv"] if par == 0 else s["accv2"]
        outbuf = s["mwov"] if par == 0 else s["mwov2"]
        wdsem = "d_tw" if par == 0 else "d_twp"
        osem = "d_mwo" if par == 0 else "d_mwop"
        slots = [s["twv"][:, r * D:(r + 1) * D] for r in range(4)]
        cx.wait_at(q, "v", bk["acc_done"][par])
        cx.dma(q, acc, wsrc, wdsem)
        wsem, wcnt = wdsem, cx.cnt[wdsem]
        i0 = bk["bld_idx"]
        v0 = cx.cnt["v"]
        for e in range(8):
            i = i0 + e
            r = i % 4
            if i < 4:
                cx.wait_at(q, "v", bk["pa_end_v"])
                cx.wait_at(q, "a", bk["pa_end_a"])
            elif e < 4:
                cx.wait_at(q, "v", bk["mac_cnt"][i - 4])
            else:
                cx.wait_at(q, "v", v0 + (e - 4) + 1)
            ssem = "d_ts%d" % r
            cx.dma(q, slots[r], twsrc[e], ssem)
            bk["slot_dma"][i] = (ssem, cx.cnt[ssem])
        for e in range(8):
            i = i0 + e
            ds, dc = bk["slot_dma"][i]
            cx.wait_at("vector", ds, dc)
            if e == 0:
                cx.wait_at("vector", wsem, wcnt)
                cx.wait_at("vector", "a", bk["gbc_a"])
            cx.wait("vector", "v")
            out = acc if e < 7 else outbuf
            if e == 7 and bk["out_sem"][par] is not None:
                osm, ocn = bk["out_sem"][par]
                cx.wait_at("vector", osm, ocn)
            cx.vop("scalar_tensor_tensor", out, slots[i % 4],
                   s["gbc"][:, e:e + 1], acc, op0=ALU.mult, op1=ALU.add)
            bk["mac_cnt"][i] = cx.cnt["v"]
        bk["bld_idx"] = i0 + 8
        done = cx.cnt["v"]
        bk["acc_done"][par] = done
        cx.wait_at(q, "v", done)
        cx.dma(q, dst, outbuf, osem)
        bk["out_sem"][par] = (osem, cx.cnt[osem])
        bk["bld_n"] += 1
        bk["group_out"][g] = (cx.cnt["d_mwo"], cx.cnt["d_mwop"])

    def emit_vector_builds(cx, n, only_m1=False):
        done = 0
        while done < n and bk["vq"] < 64:
            if only_m1 and BLD_ORDER[bk["vq"]] >= 32:
                return
            emit_build_group(cx, BLD_ORDER[bk["vq"]])
            bk["vq"] += 1
            done += 1

    def emit_attn(cx):
        cx.wait_at("tensor", "v", bk.get("hbred_v", 0))
        bk["xt_war"] = cx.cnt["t"]
        bk["qt_war"] = bk["g_done_t"]
        bk["kt_war"] = bk["g_done_t"]
        for ci in range(NCH):
            sc0 = ci * SC
            cx.wait_at("sync", "t", bk["xt_war"])
            cx.wait("sync", "d_xw0", "d_xw1")
            for hc in range(8):
                cx.dma("sync", s["xt"][:, hc * HS:(hc + 1) * HS],
                       ap["xnd"][sc0: sc0 + HS, hc * 128:(hc + 1) * 128],
                       "d_xt", transpose=True)
            if ci == 0:
                emit_halo_prologue(cx)
            else:
                cx.wait_at("vector", "t", bk["kt_war"])
                cx.wait("vector", "v")
                for hd in range(NH):
                    base = hd * HS
                    cx.vop("tensor_copy", s["kt"][:, base: base + HALO],
                           s["kt"][:, base + SC: base + SC + HALO])
                cx.wait("tensor", "v", "a")
                cx.mm(pA[0:HALO, :], s["selbf"], s["v"][:, 3 * D: 3 * D + 512],
                      start=True, stop=True)
                cx.mm(pB[0:HALO, :], s["selbf"], s["v"][:, 3 * D + 512: 4 * D],
                      start=True, stop=True)
                cx.wait("scalar", "t")
                cx.act(s["vhal"][0:HALO, 0:512], pA[0:HALO, :], AF.Copy)
                bk["ppa"].append(cx.cnt["a"])
                cx.act(s["vhal"][0:HALO, 512:1024], pB[0:HALO, :], AF.Copy)
                bk["ppa"].append(cx.cnt["a"])
            emit_qkv(cx, ci)
            bk["xt_war"] = cx.cnt["t"]
            for j in range(NT):
                gt = ci * NT + j
                if j > 0:
                    cx.wait("tensor", "a")
                    cx.mm(pA[0:HALO, :], s["selbf"], s["v"][:, (j - 1) * D:(j - 1) * D + 512],
                          start=True, stop=True)
                    cx.mm(pB[0:HALO, :], s["selbf"], s["v"][:, (j - 1) * D + 512: j * D],
                          start=True, stop=True)
                    cx.wait("scalar", "t")
                    cx.act(s["vhal"][0:HALO, 0:512], pA[0:HALO, :], AF.Copy)
                    cx.act(s["vhal"][0:HALO, 512:1024], pB[0:HALO, :], AF.Copy)
                emit_attn_tile(cx, j, gt)
                if gt >= 1:
                    emit_vector_builds(cx, 2, only_m1=True)
            bk["qt_war"] = cx.cnt["t"]
            bk["kt_war"] = cx.cnt["t"]

    def emit_halo_prologue(cx):
        cx.wait("tensor", "d_xt", "d_ld")
        cx.wait_at("tensor", "v", bk["p0ms_v"])
        cx.wait_at("scalar", "v", bk["p0ms_v"])
        cx.wait_at("tensor", "t", bk["g_done_t"])
        for hd in range(NH):
            pX = pA if hd % 2 == 0 else pB
            if hd >= 2:
                cx.wait_at("tensor", "a", bk["hraw_a"][hd - 2])
            for kt in range(8):
                cx.mm(pX[:, 0:HALO], W("wk", kt, hd),
                      s["xt"][:, kt * HS: kt * HS + HALO],
                      start=(kt == 0), stop=(kt == 7))
            cx.wait("scalar", "t")
            cx.act(s["hraw"][:, hd * HALO:(hd + 1) * HALO], pX[:, 0:HALO], AF.Copy)
            bk["ppa"].append(cx.cnt["a"])
            cx.wait("scalar", "a")
            cx.act(s["sqh"], s["hraw"][:, hd * HALO:(hd + 1) * HALO], AF.Square,
                   scale=1.0 / 32.0)
            bk["hraw_a"].append(cx.cnt["a"])
            cx.wait("tensor", "a")
            cx.mm(pE1[0:1, 0:HALO], s["onesbf"][:, 0:1], s["sqh"][:, 0:HALO],
                  start=(hd == 0), stop=(hd == 7))
        cx.wait("scalar", "t")
        cx.act(s["tmp1"][0:1, 0:HALO], pE1[0:1, 0:HALO], AF.Ln, bias=1e-6)
        cx.wait("scalar", "a")
        cx.act(s["rqrow"][0:1, 0:HALO], s["tmp1"][0:1, 0:HALO], AF.Exp, scale=-0.5)
        cx.wait("tensor", "a")
        cx.mm(pU[:, 0:HALO], s["e0f"], s["rrows"][:, 0:HALO],
              start=True, stop=True)
        cx.wait("scalar", "t")
        cx.act(s["rbh"], pU[:, 0:HALO], AF.Copy)
        cx.wait("vector", "a")
        for hd in range(NH):
            cx.vop("scalar_tensor_tensor", s["kt"][:, hd * HS: hd * HS + HALO],
                   s["hraw"][:, hd * HALO:(hd + 1) * HALO], s["kwc"][:, hd:hd + 1],
                   s["rbh"], op0=ALU.mult, op1=ALU.mult)
        cx.wait("tensor", "v", "a")
        for half, pX in ((0, pA), (1, pB)):
            for kt in range(8):
                cx.mm(pX[0:HALO, :], s["xt"][:, kt * HS: kt * HS + HALO],
                      s["wv"][:, kt * D + half * 512: kt * D + (half + 1) * 512],
                      start=(kt == 0), stop=(kt == 7))
        cx.wait("scalar", "t")
        cx.act(s["vhal"][0:HALO, 0:512], pA[0:HALO, :], AF.Copy)
        bk["ppa"].append(cx.cnt["a"])
        cx.act(s["vhal"][0:HALO, 512:1024], pB[0:HALO, :], AF.Copy)
        bk["ppa"].append(cx.cnt["a"])

    def emit_qkv(cx, ci):
        cx.wait("tensor", "d_xt", "a")
        cx.wait_at("tensor", "v", bk["p0ms_v"])
        cx.wait_at("scalar", "v", bk["p0ms_v"])
        if bk["xattn_v"]:
            cx.wait_at("tensor", "v", bk["xattn_v"][-1])
        qstt = []
        for hd in range(NH):
            pX = pA if hd % 2 == 0 else pB
            if len(bk["ppa"]) >= 2:
                cx.wait_at("tensor", "a", bk["ppa"][-2])
            for kt in range(8):
                cx.mm(pX, W("wq", kt, hd), s["xt"][:, kt * HS + HALO: kt * HS + HS],
                      start=(kt == 0), stop=(kt == 7))
            cx.wait("scalar", "t")
            if bk["kstt_v"]:
                cx.wait_at("scalar", "v", bk["kstt_v"][-1])
            cx.act(s["raw"][:, hd * SC:(hd + 1) * SC], pX, AF.Copy)
            bk["ppa"].append(cx.cnt["a"])
            cx.wait("scalar", "a")
            cx.act(s["sq"][:, (hd % 2) * SC:((hd % 2) + 1) * SC],
                   s["raw"][:, hd * SC:(hd + 1) * SC], AF.Square, scale=1.0 / 32.0)
            bk["qraw_a"].append(cx.cnt["a"])
            cx.wait("tensor", "a")
            cx.mm(pE0[0:1, :], s["onesbf"][:, 0:1],
                  s["sq"][:, (hd % 2) * SC:((hd % 2) + 1) * SC],
                  start=(hd == 0), stop=(hd == 7))
        cx.wait("scalar", "t")
        cx.act(s["tmp1"], pE0[0:1, :], AF.Ln, bias=1e-6)
        cx.wait("scalar", "a")
        cx.act(s["rqrow"], s["tmp1"], AF.Exp, scale=-0.5, bias=LNB)
        cx.wait("tensor", "a")
        cx.mm(pU[:, 0:512], s["e0f"], s["rrows"][:, 0:SC], start=True, stop=True)
        cx.wait("scalar", "t")
        cx.act(s["rbq"], pU[:, 0:512], AF.Copy)
        cx.wait_at("vector", "t", bk["qt_war"])
        cx.wait("vector", "a")
        for hd in range(NH):
            cx.vop("scalar_tensor_tensor", s["qt"][:, hd * SC:(hd + 1) * SC],
                   s["raw"][:, hd * SC:(hd + 1) * SC], s["qwc"][:, hd:hd + 1], s["rbq"],
                   op0=ALU.mult, op1=ALU.mult)
            qstt.append(cx.cnt["v"])
        for hd in range(NH):
            pX = pA if hd % 2 == 0 else pB
            if len(bk["ppa"]) >= 2:
                cx.wait_at("tensor", "a", bk["ppa"][-2])
            for kt in range(8):
                cx.mm(pX, W("wk", kt, hd), s["xt"][:, kt * HS + HALO: kt * HS + HS],
                      start=(kt == 0), stop=(kt == 7))
            cx.wait("scalar", "t")
            cx.wait_at("scalar", "v", qstt[hd])
            cx.act(s["raw"][:, hd * SC:(hd + 1) * SC], pX, AF.Copy)
            bk["ppa"].append(cx.cnt["a"])
            cx.wait("scalar", "a")
            cx.act(s["sq"][:, (hd % 2) * SC:((hd % 2) + 1) * SC],
                   s["raw"][:, hd * SC:(hd + 1) * SC], AF.Square, scale=1.0 / 32.0)
            cx.wait("tensor", "a")
            cx.mm(pE0[0:1, :], s["onesbf"][:, 0:1],
                  s["sq"][:, (hd % 2) * SC:((hd % 2) + 1) * SC],
                  start=(hd == 0), stop=(hd == 7))
        cx.wait("scalar", "t")
        cx.act(s["tmp1"], pE0[0:1, :], AF.Ln, bias=1e-6)
        cx.wait("scalar", "a")
        cx.act(s["rkrow"], s["tmp1"], AF.Exp, scale=-0.5)
        cx.wait("tensor", "a")
        cx.mm(pU[:, 512:1024], s["e0f"], s["rrows"][:, SC:2 * SC], start=True, stop=True)
        cx.wait("scalar", "t")
        cx.act(s["rbk"], pU[:, 512:1024], AF.Copy)
        cx.wait_at("vector", "t", bk["kt_war"])
        cx.wait("vector", "a")
        for hd in range(NH):
            cx.vop("scalar_tensor_tensor", s["kt"][:, hd * HS + HALO: hd * HS + HS],
                   s["raw"][:, hd * SC:(hd + 1) * SC], s["kwc"][:, hd:hd + 1], s["rbk"],
                   op0=ALU.mult, op1=ALU.mult)
            bk["kstt_v"].append(cx.cnt["v"])
        for j in range(NT):
            for half, pX in ((0, pA), (1, pB)):
                if len(bk["ppa"]) >= 2:
                    cx.wait_at("tensor", "a", bk["ppa"][-2])
                for kt in range(8):
                    cx.mm(pX, s["xt"][:, kt * HS + HALO + j * 128:
                                      kt * HS + HALO + (j + 1) * 128],
                          s["wv"][:, kt * D + half * 512: kt * D + (half + 1) * 512],
                          start=(kt == 0), stop=(kt == 7))
                cx.wait("scalar", "t")
                cx.act(s["v"][:, j * D + half * 512: j * D + (half + 1) * 512], pX, AF.Copy)
                bk["ppa"].append(cx.cnt["a"])

    def emit_attn_tile(cx, j, gt):
        bh = s["bhal0"] if gt == 0 else s["bhal"]
        if j == 0 and bk["kstt_v"]:
            cx.wait_at("tensor", "v", bk["kstt_v"][-1])
        if bk["xattn_v"]:
            cx.wait_at("tensor", "v", bk["xattn_v"][-1])
        if bk["tile_a"]:
            cx.wait_at("tensor", "a", bk["tile_a"][-1])
        cx.mm(pC, s["identbf"], s["biasm"][:, 0:512], start=True, stop=False)
        cx.mm(pD, s["identbf"], s["biasm"][:, 512:1024], start=True, stop=False)
        cx.mm(pE0[0:HALO, :], s["idhal"], bh[:, 0:512],
              start=True, stop=False)
        cx.mm(pE1[0:HALO, :], s["idhal"], bh[:, 512:1024],
              start=True, stop=False)
        for hd in range(NH):
            pX = pC if hd < 4 else pD
            pH = pE0 if hd < 4 else pE1
            col = (hd % 4) * 128
            cx.mm(pX[:, col:col + 128],
                  s["kt"][:, hd * HS + HALO + j * 128: hd * HS + HALO + (j + 1) * 128],
                  s["qt"][:, hd * SC + j * 128: hd * SC + (j + 1) * 128],
                  start=False, stop=(hd % 4 == 3))
            cx.mm(pH[0:HALO, col:col + 128],
                  s["kt"][:, hd * HS + j * 128: hd * HS + j * 128 + HALO],
                  s["qt"][:, hd * SC + j * 128: hd * SC + (j + 1) * 128],
                  start=False, stop=(hd % 4 == 3))
        cx.wait("scalar", "t")
        cx.act(s["exps"][:, 0:512], pC, AF.Exp)
        cx.act(s["exps"][:, 512:1024], pD, AF.Exp)
        cx.act(s["expsh"][0:HALO, 0:512], pE0[0:HALO, :], AF.Exp)
        cx.act(s["expsh"][0:HALO, 512:1024], pE1[0:HALO, :], AF.Exp)
        cx.wait("tensor", "a")
        for hd in range(NH):
            pZ = pE0 if hd < 4 else pE1
            col = (hd % 4) * 128
            cx.mm(pZ[0:1, col:col + 128], s["onesbf"][:, 0:1],
                  s["exps"][:, hd * 128:(hd + 1) * 128], start=True, stop=False)
            cx.mm(pZ[0:1, col:col + 128], s["ohalbf"],
                  s["expsh"][:, hd * 128:(hd + 1) * 128], start=False, stop=True)
        cx.wait("vector", "t")
        cx.vop("reciprocal", s["rz"][0:1, 0:512], pE0[0:1, :])
        cx.vop("reciprocal", s["rz"][0:1, 512:1024], pE1[0:1, :])
        cx.wait("tensor", "v")
        cx.mm(pU[:, 0:512], s["e0f"], s["rz"][:, 0:512], start=True, stop=True)
        cx.mm(pU[:, 512:1024], s["e0f"], s["rz"][:, 512:1024],
              start=True, stop=True)
        cx.wait("vector", "t")
        cx.vop("tensor_tensor", s["exps"][:, 0:512], s["exps"][:, 0:512],
               pU[:, 0:512], op=ALU.mult)
        cx.vop("tensor_tensor", s["exps"][:, 512:1024], s["exps"][:, 512:1024],
               pU[:, 512:1024], op=ALU.mult)
        cx.vop("tensor_tensor", s["expsh"][0:HALO, 0:512], s["expsh"][0:HALO, 0:512],
               pU[0:HALO, 0:512], op=ALU.mult)
        cx.vop("tensor_tensor", s["expsh"][0:HALO, 512:1024], s["expsh"][0:HALO, 512:1024],
               pU[0:HALO, 512:1024], op=ALU.mult)
        cx.wait("tensor", "v")
        for hd in range(NH):
            cx.mm(pU[:, hd * 128:(hd + 1) * 128],
                  s["v"][:, j * D + hd * 128: j * D + (hd + 1) * 128],
                  s["exps"][:, hd * 128:(hd + 1) * 128], start=True, stop=False)
            cx.mm(pU[:, hd * 128:(hd + 1) * 128],
                  s["vhal"][:, hd * 128:(hd + 1) * 128],
                  s["expsh"][:, hd * 128:(hd + 1) * 128], start=False, stop=True)
        cx.wait("scalar", "t")
        for hd in range(NH):
            cx.act(s["yt"][:, hd * 128:(hd + 1) * 128], pU[:, hd * 128:(hd + 1) * 128],
                   AF.Copy)
        bk["vhal_a"].append(cx.cnt["a"])
        cx.wait("tensor", "a")
        for half, pX in ((0, pC), (1, pD)):
            for hd in range(NH):
                cx.mm(pX, s["yt"][:, hd * 128:(hd + 1) * 128],
                      s["wo"][:, hd * D + half * 512: hd * D + (half + 1) * 512],
                      start=(hd == 0), stop=(hd == 7))
        tp = gt % 2
        if gt >= 2:
            cx.wait_at("scalar", "v", bk["xattn_v"][gt - 2])
        cx.dma("scalar", s["xr"][:, tp * D:(tp + 1) * D],
               ap["x"][gt * 128:(gt + 1) * 128, :], "d_xr%d" % tp)
        cx.wait("vector", "t", "d_xr%d" % tp)
        if gt >= 2:
            psem, pval = bk["xa_d"][gt - 2]
            cx.wait_at("vector", psem, pval)
        cx.vop("tensor_tensor", s["xattn"][:, tp * D: tp * D + 512], pC,
               s["xr"][:, tp * D: tp * D + 512], op=ALU.add)
        cx.vop("tensor_tensor", s["xattn"][:, tp * D + 512:(tp + 1) * D], pD,
               s["xr"][:, tp * D + 512:(tp + 1) * D], op=ALU.add)
        bk["xattn_v"].append(cx.cnt["v"])
        cx.wait("scalar", "v")
        xasem = "d_xa%d" % tp
        cx.dma("scalar", ap["xad"][gt * 128:(gt + 1) * 128, :],
               s["xattn"][:, tp * D:(tp + 1) * D], xasem)
        bk["xa_d"].append((xasem, cx.cnt[xasem]))
        bk["tile_a"].append(cx.cnt["a"])

    def wait_group_out(cx, who, g):
        v_cnt, p_cnt = bk["group_out"][g]
        cx.wait_at(who, "d_mwo", v_cnt)
        cx.wait_at(who, "d_mwop", p_cnt)

    def emit_ffn(cx):
        attn_t = cx.cnt["t"]
        attn_v = cx.cnt["v"]
        attn_a = cx.cnt["a"]
        for sc in range(NFC):
            if sc == 0:
                cx.wait_at("sync", "t", attn_t)
            else:
                cx.wait_at("sync", "t", bk["xtf_war"])
            for hc in range(8):
                cx.dma("sync", s["xtf"][:, hc * FCH:(hc + 1) * FCH],
                       ap["xnd"][HALO + sc * FCH: HALO + (sc + 1) * FCH,
                                 hc * 128:(hc + 1) * 128], "d_xtf", transpose=True)
            for fb in range(8):
                if sc == 0:
                    emit_vector_builds(cx, 1, only_m1=True)
                bpar = fb % 2
                wait_group_out(cx, "sync", 4 * fb + 3)
                if sc == 0 and fb == 0:
                    cx.wait_at("sync", "t", attn_t)
                elif len(bk["m1_cons"]) >= 2:
                    cx.wait_at("sync", "t", bk["m1_cons"][-2])
                for hc in range(8):
                    cx.dma("sync", s["m1t"][:, bpar * 4096 + hc * FCH:
                                            bpar * 4096 + (hc + 1) * FCH],
                           ap["m1d"][fb * 512:(fb + 1) * 512, hc * 128:(hc + 1) * 128],
                           "d_m1%d" % bpar, transpose=True)
                cx.wait("tensor", "d_m1%d" % bpar, "d_xtf")
                if sc == 0 and fb == 0:
                    cx.wait_at("tensor", "v", bk["xattn_v"][-1])
                    cx.wait_at("tensor", "a", bk["tile_a"][-1])
                if fb == 0 and bk["out_v"]:
                    cx.wait_at("tensor", "v", bk["out_v"][-1])
                for fl in range(4):
                    ft = fb * 4 + fl
                    pX = pA if ft % 2 == 0 else pB
                    if len(bk["gelu_a"]) >= 2:
                        cx.wait_at("tensor", "a", bk["gelu_a"][-2])
                    for hc in range(8):
                        cx.mm(pX, s["m1t"][:, bpar * 4096 + hc * FCH + fl * 128:
                                           bpar * 4096 + hc * FCH + (fl + 1) * 128],
                              s["xtf"][:, hc * FCH:(hc + 1) * FCH],
                              start=(hc == 0), stop=(hc == 7))
                    cx.wait("scalar", "t")
                    cx.act(s["u"][:, ft * FCH:(ft + 1) * FCH], pX, AF.Gelu_apprx_tanh,
                           bias=s["mb1"][:, ft:ft + 1])
                    bk["gelu_a"].append(cx.cnt["a"])
                bk["m1_cons"].append(cx.cnt["t"])
            bk["xtf_war"] = cx.cnt["t"]
            if sc == 0:
                emit_vector_builds(cx, 64)
            banksets = [(pA, pB), (pC, pD), (pE0, pE1),
                        (pU[:, 0:512], pU[:, 512:1024])]
            for ft in range(32):
                mpar = ft % 2
                wait_group_out(cx, "sync", 60 + ft // 8)
                if len(bk["m2_cons"]) >= 2:
                    cx.wait_at("sync", "t", bk["m2_cons"][-2])
                cx.dma("sync", s["m2t"][:, mpar * D:(mpar + 1) * D],
                       ap["m2d"][:, ft * 128:(ft + 1) * 128], "d_m2%d" % mpar,
                       transpose=True)
                cx.wait("tensor", "d_m2%d" % mpar)
                if ft == 0:
                    cx.wait_at("tensor", "a", bk["gelu_a"][-1])
                    cx.wait_at("tensor", "v", bk["xattn_v"][-1])
                    cx.wait_at("tensor", "v", bk["m2row_v"])
                    for ss in range(4):
                        p0, p1 = banksets[ss]
                        cx.mm(p0, s["e0bf"], s["m2rowb"][:, 0:512],
                              start=True, stop=False)
                        cx.mm(p1, s["e0bf"], s["m2rowb"][:, 512:1024],
                              start=True, stop=False)
                for ss in range(4):
                    p0, p1 = banksets[ss]
                    cx.mm(p0, s["u"][:, ft * FCH + ss * 128: ft * FCH + (ss + 1) * 128],
                          s["m2t"][:, mpar * D: mpar * D + 512],
                          start=False, stop=(ft == 31))
                    cx.mm(p1, s["u"][:, ft * FCH + ss * 128: ft * FCH + (ss + 1) * 128],
                          s["m2t"][:, mpar * D + 512:(mpar + 1) * D],
                          start=False, stop=(ft == 31))
                bk["m2_cons"].append(cx.cnt["t"])
            for ss in range(4):
                gt = sc * 4 + ss
                p0, p1 = banksets[ss]
                tp = gt % 2
                xasem, xaval = bk["xa_d"][gt]
                cx.wait_at("scalar", xasem, xaval)
                if gt < 2:
                    cx.wait_at("scalar", "v", bk["xattn_v"][-1])
                if gt >= 2:
                    cx.wait_at("scalar", "v", bk["out_v"][gt - 2])
                cx.dma("scalar", s["xf"][:, tp * D:(tp + 1) * D],
                       ap["xad"][gt * 128:(gt + 1) * 128, :], "d_xf%d" % tp)
                cx.wait("vector", "t", "d_xf%d" % tp)
                if gt >= 2:
                    psem, pval = bk["out_d"][gt - 2]
                    cx.wait_at("vector", psem, pval)
                cx.vop("tensor_tensor", s["outb"][:, tp * D: tp * D + 512], p0,
                       s["xf"][:, tp * D: tp * D + 512], op=ALU.add)
                cx.vop("tensor_tensor", s["outb"][:, tp * D + 512:(tp + 1) * D], p1,
                       s["xf"][:, tp * D + 512:(tp + 1) * D], op=ALU.add)
                bk["out_v"].append(cx.cnt["v"])
                cx.wait("scalar", "v")
                osem = "d_out%d" % tp
                cx.dma("scalar", ap["out"][gt * 128:(gt + 1) * 128, :],
                       s["outb"][:, tp * D:(tp + 1) * D], osem)
                bk["out_d"].append((osem, cx.cnt[osem]))

    PHASES = int(os.environ.get("K_PHASES", "99"))

    def emit_dummy_out(cx):
        dump = os.environ.get("K_DUMP", "")
        if dump == "xn":
            cx.wait("gpsimd", "d_xw0", "d_xw1")
            cx.dma("gpsimd", ap["out"], ap["xnd"][HALO:HALO + OWN, :], "d_out0")
            cx.wait("gpsimd", "d_out0")
            return
        if dump == "xt":
            cx.wait("gpsimd", "d_xt", "t")
            for r in range(8):
                cx.dma("gpsimd", ap["out"][r * 128:(r + 1) * 128, 0:HS],
                       s["xt"][:, r * HS:(r + 1) * HS], "d_out0")
            cx.wait("gpsimd", "d_out0")
            return
        if dump == "gb1r":
            cx.wait("gpsimd", "d_ldg", "d_ld")
            cx.dma("gpsimd", ap["out"][0:1, 0:D], s["gb1r"], "d_out0")
            cx.wait("gpsimd", "d_out0")
            return
        if dump == "gw1t":
            cx.wait("gpsimd", "d_ldg", "d_ld")
            for r in range(8):
                cx.dma("gpsimd", ap["out"][r * 128:(r + 1) * 128, 0:1024],
                       s["gw1t"][:, r * 1024:(r + 1) * 1024], "d_out0")
            cx.dma("gpsimd", ap["out"][1024:1152, 0:1024],
                   s["wq"][:, 0:1024], "d_out0")
            cx.wait("gpsimd", "d_out0")
            return
        if dump == "hrelu":
            cx.wait("gpsimd", "a", "v", "t")
            cx.dma("gpsimd", ap["out"][0:128, 0:D], s["hrelu"], "d_out0")
            cx.dma("gpsimd", ap["out"][128:256, 0:8], s["hbacc"], "d_out0")
            cx.dma("gpsimd", ap["out"][256:384, 0:256], s["junk1"][:, 0:256], "d_out0")
            cx.dma("gpsimd", ap["out"][384:512, 0:128], s["onesbf"], "d_out0")
            cx.dma("gpsimd", ap["out"][512:640, 0:128], s["e0bf"], "d_out0")
            cx.dma("gpsimd", ap["out"][640:768, 0:128], s["identbf"], "d_out0")
            cx.dma("gpsimd", ap["out"][768:896, 0:128], s["onesf"], "d_out0")
            cx.dma("gpsimd", ap["out"][896:1024, 0:128], s["e0f"], "d_out0")
            cx.dma("gpsimd", ap["out"][1024:1152, 0:32], s["idhal"], "d_out0")
            cx.dma("gpsimd", ap["out"][1280:1408, 0:32], s["selbf"], "d_out0")
            cx.wait("gpsimd", "d_out0")
            return
        if dump == "xad":
            cx.wait("gpsimd", "d_xa0", "d_xa1")
            cx.dma("gpsimd", ap["out"], ap["xad"], "d_out0")
            cx.wait("gpsimd", "d_out0")
            return
        if dump == "hbar":
            cx.wait("gpsimd", "a", "v", "t")
            cx.dma("gpsimd", ap["out"][0:128, 0:8], s["hbarT"], "d_out0")
            cx.dma("gpsimd", ap["out"][128:129, 0:8], s["grow"][0:1, :], "d_out0")
            cx.dma("gpsimd", ap["out"][129:257, 0:8], s["gbc"], "d_out0")
            cx.dma("gpsimd", ap["out"][257:385, 0:32], s["mb1"], "d_out0")
            cx.wait("gpsimd", "d_out0")
            return
        for gt in range(16):
            cx.wait("vector", "a", "t")
            cx.vop("memset", s["outb"][:, 0:D], 0.0)
            cx.wait("scalar", "v")
            cx.dma("scalar", ap["out"][gt * 128:(gt + 1) * 128, :],
                   s["outb"][:, 0:D], "d_out0")
        cx.wait("scalar", "d_out0")

    def full_schedule(cx):
        emit_p0(cx)
        if PHASES >= 2:
            emit_ln_gate(cx)
        bk["pa_end_v"] = cx.cnt["v"]
        bk["pa_end_a"] = cx.cnt["a"]
        if PHASES >= 3:
            emit_g(cx)
        if PHASES >= 4:
            emit_attn(cx)
        if PHASES >= 5:
            emit_ffn(cx)
            cx.wait("scalar", "d_out0", "d_out1")
        else:
            emit_dummy_out(cx)

    with nc.Block() as block:
        def run_for(name):
            def body(eng):
                bk.clear()
                bk.update({
                    "gw1ev": [], "xband_v": [], "xband_a": [], "xw_d": [],
                    "relu_a": [], "hbar_t": [], "xt_gate_t": [],
                    "group_out": {},
                    "bld_n": 0, "bld_idx": 0, "slot_dma": {}, "mac_cnt": {},
                    "acc_done": {0: 0, 1: 0}, "out_sem": {0: None, 1: None},
                    "vq": 0, "hraw_a": [], "qraw_a": [], "kstt_v": [], "ppa": [],
                    "hbacc_v": [],
                    "qt_war": 0, "kt_war": 0, "xt_war": 0,
                    "vhal_a": [], "xattn_v": [], "xa_d": [], "tile_a": [],
                    "m1_cons": [], "m2_cons": [], "gelu_a": [], "xtf_war": 0,
                    "out_v": [], "out_d": [],
                })
                c.reset(name, eng)
                full_schedule(c)
            return body
        block.sync(run_for("sync"))
        block.tensor(run_for("tensor"))
        block.vector(run_for("vector"))
        block.scalar(run_for("scalar"))
        block.gpsimd(run_for("gpsimd"))

    stack.close()
    return nc


_NC_CACHE = {}

def _consts():
    ident = np.eye(128)
    onesm = np.ones((128, 128))
    sel = np.zeros((128, HALO))
    for p in range(HALO):
        sel[128 - HALO + p, p] = 1.0
    slopes = np.array([0.5 ** i for i in range(NH)])
    q_idx = np.arange(128)
    rel_m = np.arange(128)[:, None] - q_idx[None, :]
    valid_m = (rel_m >= -W_ATT) & (rel_m <= 0)
    rel_h = (np.arange(HALO)[:, None] - HALO) - q_idx[None, :]
    valid_h = (rel_h >= -W_ATT) & (rel_h <= 0)
    biasm = np.empty((128, NH * 128), dtype=np.float32)
    bhal = np.empty((HALO, NH * 128), dtype=np.float32)
    for hd in range(NH):
        biasm[:, hd * 128:(hd + 1) * 128] = np.where(valid_m, slopes[hd] * rel_m, NEG)
        bhal[:, hd * 128:(hd + 1) * 128] = np.where(valid_h, slopes[hd] * rel_h, NEG)
    e0 = np.zeros((128, 128)); e0[0, :] = 1.0
    idh = np.zeros((128, HALO)); idh[np.arange(HALO), np.arange(HALO)] = 1.0
    ohal = np.zeros((128, 1)); ohal[0:HALO, 0] = 1.0
    bhal_p = np.zeros((128, NH * 128), dtype=np.float32); bhal_p[0:HALO] = bhal
    bhal0_p = np.zeros((128, NH * 128), dtype=np.float32); bhal0_p[0:HALO] = NEG
    return {
        "identbf": ident.astype(BF), "identf": ident.astype(np.float32),
        "onesbf": onesm.astype(BF), "onesf": onesm.astype(np.float32),
        "e0bf": e0.astype(BF), "e0f": e0.astype(np.float32),
        "idhal": idh.astype(BF), "ohalbf": ohal.astype(BF),
        "selbf": sel.astype(BF),
        "biasm": biasm.astype(BF), "bhal": bhal_p.astype(BF),
        "bhal0_neg": bhal0_p.astype(BF),
    }


def kernel(**inputs):
    cst = _consts()
    if "nc" not in _NC_CACHE:
        _NC_CACHE["nc"] = build_program()
    nc = _NC_CACHE["nc"]

    f = lambda k: np.ascontiguousarray(np.asarray(inputs[k], dtype=np.float32))
    x = f("L0_attn_x")
    Tb1 = f("L0_ffn_Tb1")
    Tb2 = f("L0_ffn_Tb2")
    Tb1c = np.ascontiguousarray(
        Tb1.reshape(8, 32, 128).transpose(2, 0, 1).reshape(128, 8 * 32))
    Tb2c = np.ascontiguousarray(
        Tb2.reshape(8, 8, 128).transpose(2, 0, 1).reshape(128, 8 * 8))
    gw1t = np.ascontiguousarray(
        f("L0_ffn_gate_w1").T.reshape(8, 128, D).transpose(1, 0, 2).reshape(128, 8 * D))
    gw2t = np.ascontiguousarray(
        f("L0_ffn_gate_w2").T.reshape(8, 128, E).transpose(1, 0, 2).reshape(128, 8 * E))
    shared = {
        "wq": f("L0_attn_wq"), "wk": f("L0_attn_wk"), "wv": f("L0_attn_wv"),
        "wo": f("L0_attn_wo"),
        "qwc": np.ascontiguousarray(f("L0_attn_q_norm_w").reshape(8, 128).T),
        "kwc": np.ascontiguousarray(f("L0_attn_k_norm_w").reshape(8, 128).T),
        "gw1tc": gw1t, "gw2tc": gw2t,
        "gb1": np.ascontiguousarray(f("L0_ffn_gate_b1").reshape(1, D)),
        "gb2r": np.ascontiguousarray(f("L0_ffn_gate_b2").reshape(1, E)),
        "W1": f("L0_ffn_W1"), "W2": f("L0_ffn_W2"),
        "TW1": f("L0_ffn_TW1"), "TW2": f("L0_ffn_TW2"),
        "b1c": np.ascontiguousarray(f("L0_ffn_b1").reshape(32, 128).T),
        "b2c": np.ascontiguousarray(f("L0_ffn_b2").reshape(8, 128).T),
        "Tb1c": Tb1c, "Tb2c": Tb2c,
        "identbf": cst["identbf"],
        "onesbf": cst["onesbf"], "onesf": cst["onesf"], "selbf": cst["selbf"],
        "e0bf": cst["e0bf"], "e0f": cst["e0f"],
        "idhal": cst["idhal"], "ohalbf": cst["ohalbf"],
        "biasm": cst["biasm"], "bhal": cst["bhal"],
        "epsc": np.tile(np.array([[1e-5, 1e-6]], dtype=np.float32), (128, 1)),
        "epsc3": np.full((128, 1), -0.5 * math.log(HD), dtype=np.float32),
    }
    in_maps = []
    for core in range(8):
        b, t = core // 2, core % 2
        m = dict(shared)
        xb = x[b]
        m["x"] = np.ascontiguousarray(np.concatenate([xb[t * OWN:], xb[:t * OWN]], axis=0))
        m["xhalo"] = (np.zeros((HALO, D), dtype=np.float32) if t == 0
                      else np.ascontiguousarray(xb[OWN - HALO: OWN]))
        m["bhal0"] = cst["bhal0_neg"] if t == 0 else cst["bhal"]
        in_maps.append(m)

    res = run_bass_kernel_spmd(nc, in_maps, core_ids=list(range(8)))
    out = np.empty((B, T, D), dtype=np.float32)
    for core in range(8):
        b, t = core // 2, core % 2
        out[b, t * OWN:(t + 1) * OWN] = np.asarray(res.results[core]["out"])
    return out
